# version 18
# baseline (speedup 1.0000x reference)
BISECT1 = "gpsimd"
"""Fused single-launch megakernel for mk_seq_postln_d1 (Mamba block + ternary
FFN, depth 1, x: [2, 8192, 1024]) on 8 trn2 NeuronCores.

Parallelization
  - 2 sequences x 4-core groups. Within a group each core owns a 512-wide
    slice of the DI=2048 inner dim for the whole mamba part (in_proj slice,
    conv, selective scan) over the full 8192-token sequence, so the scan has
    no cross-core sequential chain.
  - Each core owns the q-th 256-token quarter of every 1024-token chunk for
    out_proj / layernorms / FFN (2048 tokens per core).
  - Cross-core data: two AllGather collectives per chunk inside each 4-core
    group: bcd partials [96 x 1024] fp32 and ybar slices [512 x 1024] bf16.
Layouts
  - mamba inner tensors feature-major [d on partitions x tokens free]; scan
    via tensor_tensor_scan along free dim with per-(d-chunk, n) state column.
  - residual / LN / FFN-output pipeline token-major; FFN hidden feature-major.
  - bf16 matmuls with fp32 PSUM accumulation, fp32 elementwise + scan state.
"""

import numpy as np
import concourse.bass as bass
import concourse.mybir as mybir
from concourse.bass_utils import run_bass_kernel_spmd
from contextlib import ExitStack

F32 = mybir.dt.float32
BF16 = mybir.dt.bfloat16
I32 = mybir.dt.int32
AF = mybir.ActivationFunctionType
ALU = mybir.AluOpType
AXL = mybir.AxisListType
ET = mybir.EngineType

B, L, D = 2, 8192, 1024
DI, NST, DR, KC = 2048, 16, 64, 4
DH = 4096
EPS = 1e-5
LC = 512
NCHUNK = L // LC
NSUB = LC // 512
DSL = 512
NDC = DSL // 128
QT = LC // 4
NTOK = NCHUNK * QT
FSUB = 1024
NFS = NTOK // FSUB

ENGS = ("sync", "scalar", "vector", "tensor", "gpsimd")

EMIT_CHUNKS = NCHUNK
DBG_CI = 1
EMIT_FFN = True
DEBUG = False
SERIAL = False


PSLOT = {"sync": 6, "scalar": 1, "gpsimd": 6}
DMAQ = ("sync", "scalar", "gpsimd")
CHANS = tuple((e, "c") for e in ENGS) + tuple(
    (e, "d", i) for e in DMAQ for i in range(max(6, *PSLOT.values()))) + (
    ("gpsimd", "cc"),)


class Prog:

    def __init__(self):
        self.ops = []
        self.cum = {ch: 0 for ch in CHANS}
        self.last_w = {}
        self.readers = {}
        self.fence_lv = {ch: 0 for ch in CHANS}
        self.dma_i = {e: 0 for e in DMAQ}
        self.unsafe = set()

    @staticmethod
    def _name(b):
        if isinstance(b, str):
            return b
        n = getattr(b, "name", None)
        if n is not None:
            return n
        return b.tensor.name

    def op(self, eng, fn, reads=(), writes=(), dma=False, fifo_unsafe=False,
           cc=False):
        if dma:
            i = self.dma_i[eng]
            self.dma_i[eng] = i + 1
            ch = (eng, "d", i % PSLOT[eng])
            inc = 16
        elif cc:
            ch = (eng, "cc")
            inc = 1
        else:
            ch = (eng, "c")
            inc = 1
        waits = {}

        def add(sch, sv):
            if sv <= 0:
                return
            if sch == (eng, "c") and (sch, sv) not in self.unsafe:
                return
            waits[sch] = max(waits.get(sch, 0), sv)

        for b in reads:
            lw = self.last_w.get(self._name(b))
            if lw:
                add(*lw)
        for b in writes:
            nm = self._name(b)
            lw = self.last_w.get(nm)
            if lw:
                add(*lw)
            for r in self.readers.get(nm, ()):
                add(*r)
        for sch, lv in self.fence_lv.items():
            add(sch, lv)
        if dma:
            waits[ch] = max(waits.get(ch, 0), self.cum[ch])
        self.cum[ch] += inc
        me = (ch, self.cum[ch])
        if not dma and eng in ("vector", "gpsimd"):
            self.unsafe.add(me)
        for b in reads:
            self.readers.setdefault(self._name(b), []).append(me)
        for b in writes:
            nm = self._name(b)
            self.last_w[nm] = me
            self.readers[nm] = []
        self.ops.append((eng, fn, inc, ch, waits))

    def fence(self):
        self.fence_lv = dict(self.cum)

    def close_fence(self, names):
        for nm in names:
            for t in ([self.last_w[nm]] if nm in self.last_w else []) +                      list(self.readers.get(nm, ())):
                ch, cum = t
                if self.fence_lv.get(ch, 0) < cum:
                    self.fence_lv[ch] = cum

    def finalize(self):
        covered = {e: {ch: 0 for ch in CHANS} for e in ENGS}
        streams = {e: [] for e in ENGS}
        for eng, fn, inc, ch, waits in self.ops:
            w2 = []
            for sch, sv in sorted(waits.items(), key=lambda kv: str(kv[0])):
                if sv > covered[eng][sch]:
                    w2.append((sch, sv))
                    covered[eng][sch] = sv
            streams[eng].append((fn, inc, ch, w2))
        return streams


def build(debug=False):
    nc = bass.Bass("TRN2", target_bir_lowering=False)

    xin = nc.dram_tensor("x_seq", [L, D], F32, kind="ExternalInput").ap()
    x_q = nc.dram_tensor("x_q", [NCHUNK, QT, D], F32, kind="ExternalInput").ap()
    wi = nc.dram_tensor("wi", [D, 1024], F32, kind="ExternalInput").ap()
    convw = nc.dram_tensor("convw", [128, NDC * KC], F32, kind="ExternalInput").ap()
    convb = nc.dram_tensor("convb", [128, NDC], F32, kind="ExternalInput").ap()
    bcdw = nc.dram_tensor("bcdw", [DSL, 96], F32, kind="ExternalInput").ap()
    dupw = nc.dram_tensor("dupw", [DR, DSL], F32, kind="ExternalInput").ap()
    dupb = nc.dram_tensor("dupb", [128, NDC], F32, kind="ExternalInput").ap()
    lmA = nc.dram_tensor("lmA", [128, NDC * NST], F32, kind="ExternalInput").ap()
    dparam = nc.dram_tensor("dparam", [128, NDC], F32, kind="ExternalInput").ap()
    rmsw = nc.dram_tensor("rmsw", [128, 8], F32, kind="ExternalInput").ap()
    wout = nc.dram_tensor("wout", [DI, D], F32, kind="ExternalInput").ap()
    w1t = nc.dram_tensor("w1t", [D, DH], F32, kind="ExternalInput").ap()
    w2t = nc.dram_tensor("w2t", [D, DH], F32, kind="ExternalInput").ap()
    h1t = nc.dram_tensor("h1t", [DH, DH], F32, kind="ExternalInput").ap()
    h2t = nc.dram_tensor("h2t", [DH, DH], F32, kind="ExternalInput").ap()
    w3t = nc.dram_tensor("w3t", [DH, D], F32, kind="ExternalInput").ap()
    w1s = nc.dram_tensor("w1s", [128, 32], F32, kind="ExternalInput").ap()
    w2s = nc.dram_tensor("w2s", [128, 32], F32, kind="ExternalInput").ap()
    h1s = nc.dram_tensor("h1s", [128, 32], F32, kind="ExternalInput").ap()
    h2s = nc.dram_tensor("h2s", [128, 32], F32, kind="ExternalInput").ap()
    w3s = nc.dram_tensor("w3s", [1, D], F32, kind="ExternalInput").ap()
    ident_in = nc.dram_tensor("ident_in", [128, 128], F32, kind="ExternalInput").ap()
    out = nc.dram_tensor("out", [NTOK, D], F32, kind="ExternalOutput").ap()

    cc_bcd_in = nc.dram_tensor("cc_bcd_in", [96, LC], F32, kind="Internal").ap()
    cc_bcd_out = nc.dram_tensor("cc_bcd_out", [4 * 96, LC], F32, kind="Internal").ap()
    cc_y_in2 = [nc.dram_tensor(f"cc_y_in{j}", [DSL, LC], BF16,
                               kind="Internal").ap() for j in range(2)]
    cc_y_out2 = [nc.dram_tensor(f"cc_y_out{j}", [DI, LC], BF16,
                                kind="Internal").ap() for j in range(2)]
    bc_scr2 = [nc.dram_tensor(f"bc_scr{j}", [16, 2 * LC], F32,
                               kind="Internal").ap() for j in range(2)]
    x1_spill = nc.dram_tensor("x1_spill", [NTOK, D], F32, kind="Internal").ap()
    x1b_spill = nc.dram_tensor("x1b_spill", [NTOK, D], BF16, kind="Internal").ap()

    dbg = {}
    if debug:
        for nme, shape, dt in [
            ("d_xn", [128, 8 * 512], BF16),
            ("d_xc", [128, NDC * LC], F32),
            ("d_bcd", [96, LC], F32),
            ("d_delta", [128, NDC * LC], F32),
            ("d_yacc", [128, NDC * LC], F32),
            ("d_x1", [QT, D], F32),
            ("d_yall", [128, 16 * QT], BF16),
            ("d_x1s", [NTOK, D], BF16),
            ("d_g", [128, 32 * FSUB], BF16),
            ("d_h1", [128, 32 * FSUB], BF16),
            ("d_h2", [128, 32 * FSUB], BF16),
            ("d_z2", [128, 8 * D], F32),
            ("d_st", [128, 8], F32),
            ("d_st2", [128, 8], F32),
            ("d_rs", [128, 8], F32),
            ("d_state", [128, NDC * NST], F32),
            ("d_zb", [128, 2 * D], F32),
        ]:
            dbg[nme] = nc.dram_tensor(nme, shape, dt, kind="ExternalOutput").ap()

    pid = nc.partition_id([ET.SP])
    qv0 = (pid % 4) * QT
    qrow0 = pid % 4

    p = Prog()

    def OP(eng, reads, writes, fn, dma=False, cc=False):
        p.op(eng, fn, reads=reads, writes=writes, dma=dma, cc=cc)
        if SERIAL:
            p.fence()

    def DMA(eng, out_ap, in_ap, reads, writes):
        OP(eng, reads, writes,
           lambda e, o=out_ap, i=in_ap: e.dma_start(out=o, in_=i), dma=True)

    def tracked(es, names):
        def sb(nme, shape, dt=F32):
            names.append(nme)
            return es.enter_context(nc.sbuf_tensor(nme, shape, dt))
        return sb

    with ExitStack() as es0:
        sb0 = lambda nme, shape, dt=F32: es0.enter_context(nc.sbuf_tensor(nme, shape, dt))
        ps0 = lambda nme, shape, dt=F32: es0.enter_context(nc.psum_tensor(nme, shape, dt))

        bcdw_sb = sb0("bcdw_sb", [128, NDC, 96], BF16)
        dupw_sb = sb0("dupw_sb", [DR, DSL], BF16)
        convw_sb = sb0("convw_sb", [128, NDC * KC])
        convb_sb = sb0("convb_sb", [128, NDC])
        dupb_sb = sb0("dupb_sb", [128, NDC])
        lmA_sb = sb0("lmA_sb", [128, NDC * NST])
        A_sb = sb0("A_sb", [128, NDC * NST])
        dparam_sb = sb0("dparam_sb", [128, NDC])
        rmsw_sb = sb0("rmsw_sb", [128, 8])
        w1s_sb = sb0("w1s_sb", [128, 32])
        w2s_sb = sb0("w2s_sb", [128, 32])
        h1s_sb = sb0("h1s_sb", [128, 32])
        h2s_sb = sb0("h2s_sb", [128, 32])
        w3sRep = sb0("w3sRep", [128, D])
        ident = sb0("ident", [128, 128], BF16)
        state = sb0("state", [128, NDC * NST])
        dumpb = sb0("dumpb", [128, D], BF16)
        sqc = sb0("sqc", [128, 4])
        rstd = sb0("rstd", [128, 4])
        s1 = sb0("s1", [128, 8])
        s2 = sb0("s2", [128, 8])
        meanb = sb0("meanb", [128, 8])
        varb = sb0("varb", [128, 8])
        rstd1 = sb0("rstd1", [128, 8])

        p_tr = [ps0(f"p_tr{i}", [128, 1024], BF16) for i in range(2)]
        p_mm = [ps0(f"p_mm{i}", [128, 512]) for i in range(5)]
        p_bcd = ps0("p_bcd", [96, 512])

        sems = {ch: es0.enter_context(
            nc.semaphore("sem_" + "_".join(str(x) for x in ch)))
            for ch in CHANS}
        block = es0.enter_context(nc.Block())

        DMA("gpsimd", bcdw_sb[:], bcdw.rearrange("(c p) m -> p c m", p=128),
            ["bcdw"], [bcdw_sb])
        DMA("gpsimd", dupw_sb[:], dupw, ["dupw"], [dupw_sb])
        DMA("sync", convw_sb[:], convw, ["convw"], [convw_sb])
        DMA("sync", convb_sb[:], convb, ["convb"], [convb_sb])
        DMA("sync", dupb_sb[:], dupb, ["dupb"], [dupb_sb])
        DMA("sync", lmA_sb[:], lmA, ["lmA"], [lmA_sb])
        DMA("sync", dparam_sb[:], dparam, ["dparam"], [dparam_sb])
        DMA("sync", rmsw_sb[:], rmsw, ["rmsw"], [rmsw_sb])
        DMA("sync", w1s_sb[:], w1s, ["w1s"], [w1s_sb])
        DMA("sync", w2s_sb[:], w2s, ["w2s"], [w2s_sb])
        DMA("sync", h1s_sb[:], h1s, ["h1s"], [h1s_sb])
        DMA("sync", h2s_sb[:], h2s, ["h2s"], [h2s_sb])
        DMA("scalar", w3sRep[:], w3s[0:1, :].partition_broadcast(128),
            ["w3s"], [w3sRep])
        OP("gpsimd", [], [state], lambda e: e.memset(state[:], 0.0))
        snapped = {}

        def snap_vals(e):
            snapped["qv"] = e.snap(qv0, min_val=0, max_val=3 * QT)
            return e.nop()
        OP("sync", [], [], snap_vals)
        DMA("gpsimd", ident[:], ident_in, ["ident_in"], [ident])
        OP("scalar", [lmA_sb], [A_sb],
           lambda e: e.activation(out=A_sb[:], in_=lmA_sb[:], func=AF.Exp))
        OP("scalar", [A_sb], [A_sb],
           lambda e: e.mul(A_sb[:], A_sb[:], -1.0))
        constv = [EPS, 1.0, -1.0, -0.5, 2.0, 0.5, 1.0 / D, 0.0]
        const_sb = sb0("const_sb", [128, len(constv)])
        for i, v in enumerate(constv):
            OP("gpsimd", [], [const_sb],
               lambda e, i=i, v=v: e.memset(const_sb[:, i:i + 1], v))
            nc.const_aps.register(F32, v, const_sb[:, i:i + 1])
        p.fence()

        with ExitStack() as esM:
            nmM = []
            sbM = tracked(esM, nmM)
            xs_sub = sbM("xs_sub", [128, NDC, 515])
            xc2 = [sbM(f"xc{j}", [128, NDC, LC]) for j in range(2)]
            delta2 = [[sbM(f"delta{j}_{dc}", [128, LC]) for dc in range(NDC)]
                      for j in range(2)]
            u_dc = [sbM(f"u{dc}", [128, LC]) for dc in range(NDC)]
            y_dc = [sbM(f"yac{dc}", [128, LC]) for dc in range(NDC)]
            bcd2 = [sbM(f"bcd{j}", [96, LC]) for j in range(2)]
            sres2 = [sbM(f"sres{j}", [128, NDC, LC], BF16) for j in range(2)]
            dlr2 = [sbM("dlr0", [DR, LC], BF16)] * 2
            x_tm = sbM("x_tm", [128, 4, D])
            xhat2 = [sbM(f"xhat{i}", [128, D], BF16) for i in range(2)]
            xn_fm = sbM("xn_fm", [128, 8, 512], BF16)
            xcb = sbM("xcb", [128, NDC, 512], BF16)
            conv_t2 = [sbM(f"conv_t{i}", [128, 512]) for i in range(2)]
            conv_t = conv_t2[0]
            wi_t = sbM("wi_t", [128, 8, 1024], BF16)
            sp_t1 = sbM("sp_t1", [128, 512])
            sp_t2 = conv_t
            bmcm2 = [sbM(f"bmcm{i}", [128, 2, LC]) for i in range(2)]
            abuf = [sbM(f"abuf{i}", [128, LC]) for i in range(2)]
            bxbuf = [sbM(f"bxbuf{i}", [128, LC]) for i in range(2)]
            hbuf = [sbM(f"hbuf{i}", [128, LC]) for i in range(2)]
            wwbuf = [sbM(f"wwbuf{i}", [128, LC]) for i in range(2)]
            ybar_t = sbM("ybar_t", [128, LC], BF16)
            wout_t = sbM("wout_t", [128, 16, D], BF16)
            yall = sbM("yall", [128, 16, QT], BF16)
            x_qt = sbM("x_qt", [128, 1, D])
            zb = sbM("zb", [128, 1, D])
            x1bb = sbM("x1bb", [128, 1, D], BF16)
            if debug:
                print("sbuf watermark after mamba decls:", nc.sbuf_base,
                      "top:", nc.sbuf_top)
            OP("gpsimd", [], [xs_sub], lambda e: e.memset(xs_sub[:, :, 0:3], 0.0))
            DMA("gpsimd", wi_t[:], wi.rearrange("(k p) m -> p k m", p=128),
                ["wi"], [wi_t])
            DMA("gpsimd", wout_t[:], wout.rearrange("(k p) m -> p k m", p=128),
                ["wout"], [wout_t])

            def genA(ci):
                T0 = ci * LC
                xc = xc2[ci % 2]
                bcd = bcd2[ci % 2]
                sres = sres2[ci % 2]
                DMA("sync", x_tm[:],
                    xin[T0:T0 + 512, :].rearrange("(j p) d -> p j d", p=128),
                    ["x_seq"], [x_tm])
                yield
                for j in range(4):
                    OP("scalar", [x_tm], [sqc, dumpb],
                       lambda e, j=j: e.activation(
                           out=dumpb[:], in_=x_tm[:, j], func=AF.Square,
                           accum_out=sqc[:, j:j + 1]))
                OP("scalar", [sqc], [rstd],
                   lambda e: e.activation(out=rstd[:], in_=sqc[:],
                                          func=AF.Ln, scale=1.0 / D, bias=EPS))
                OP("scalar", [rstd], [rstd],
                   lambda e: e.activation(out=rstd[:], in_=rstd[:],
                                          func=AF.Exp, scale=-0.5))
                yield
                for j in range(4):
                    xh = xhat2[j % 2]
                    OP("scalar", [x_tm, rstd], [xh],
                       lambda e, j=j, xh=xh: e.activation(
                           out=xh[:], in_=x_tm[:, j], func=AF.Identity,
                           scale=rstd[:, j:j + 1]))
                    for dc8 in range(8):
                        pt = p_tr[dc8 % 2]
                        OP("tensor", [xh, ident], [pt],
                           lambda e, dc8=dc8, pt=pt, xh=xh: e.transpose(
                               pt[:, 0:128],
                               xh[:, dc8 * 128:(dc8 + 1) * 128],
                               ident[:]))
                        OP("scalar", [pt, rmsw_sb], [xn_fm],
                           lambda e, j=j, dc8=dc8, pt=pt: e.activation(
                               out=xn_fm[:, dc8, j * 128:(j + 1) * 128],
                               in_=pt[:, 0:128], func=AF.Identity,
                               scale=rmsw_sb[:, dc8:dc8 + 1]))
                    yield
                def mk_mm(m):
                    pm = p_mm[m % 2]

                    def mmg(e, m=m, pm=pm):
                        r = None
                        for k in range(8):
                            r = e.matmul(
                                pm[:], wi_t[:, k, m * 128:(m + 1) * 128],
                                xn_fm[:, k, :], start=(k == 0), stop=(k == 7))
                        return r
                    OP("tensor", [wi_t, xn_fm], [pm], mmg)

                def mk_evac(m):
                    pm = p_mm[m % 2]
                    if m < 4:
                        OP("scalar", [pm], [xs_sub],
                           lambda e, m=m, pm=pm: e.activation(
                               out=xs_sub[:, m, 3:515], in_=pm[:],
                               func=AF.Identity))
                    else:
                        OP("scalar", [pm], [sres],
                           lambda e, m=m, pm=pm, sres=sres: e.activation(
                               out=sres[:, m - 4, :], in_=pm[:],
                               func=AF.Silu))

                def mk_conv(dc):
                    ct = conv_t2[dc % 2]
                    OP("vector", [xs_sub, convw_sb], [ct],
                       lambda e, dc=dc, ct=ct: e.tensor_scalar(
                           out=ct[:], in0=xs_sub[:, dc, 0:512],
                           scalar1=convw_sb[:, dc * KC:dc * KC + 1],
                           scalar2=None, op0=ALU.mult))
                    for k in range(1, KC):
                        OP("vector", [xs_sub, convw_sb, ct], [ct],
                           lambda e, dc=dc, k=k, ct=ct: e.scalar_tensor_tensor(
                               out=ct[:], in0=xs_sub[:, dc, k:k + 512],
                               scalar=convw_sb[:, dc * KC + k:dc * KC + k + 1],
                               in1=ct[:], op0=ALU.mult, op1=ALU.add))

                def mk_silu(dc):
                    ct = conv_t2[dc % 2]
                    OP("scalar", [ct, convb_sb], [xc],
                       lambda e, dc=dc, xc=xc, ct=ct: e.activation(
                           out=xc[:, dc, :], in_=ct[:],
                           func=AF.Silu, bias=convb_sb[:, dc:dc + 1]))
                    OP("scalar", [xc], [xcb],
                       lambda e, dc=dc, xc=xc: e.activation(
                           out=xcb[:, dc, :], in_=xc[:, dc, :],
                           func=AF.Identity))

                mk_mm(0)
                mk_mm(1)
                yield
                mk_evac(0)
                mk_evac(1)
                mk_mm(2)
                mk_mm(3)
                yield
                mk_evac(2)
                mk_evac(3)
                mk_conv(0)
                mk_conv(1)
                yield
                mk_silu(0)
                mk_silu(1)
                mk_conv(2)
                mk_conv(3)
                yield
                mk_silu(2)
                mk_silu(3)
                if debug and ci == 0:
                    DMA("sync", dbg["d_xc"], xc[:].rearrange("p a l -> p (a l)"),
                        [xc], ["d_xc"])
                    DMA("sync", dbg["d_xn"],
                        xn_fm[:].rearrange("p a l -> p (a l)"),
                        [xn_fm], ["d_xn"])
                OP(BISECT1, [xs_sub], [xs_sub],
                   lambda e: e.tensor_copy(xs_sub[:, :, 0:3],
                                           xs_sub[:, :, 512:515]))

                def bcdg(e):
                    r = None
                    for dc in range(NDC):
                        r = e.matmul(p_bcd[:], bcdw_sb[:, dc, :], xcb[:, dc, :],
                                     start=(dc == 0), stop=(dc == NDC - 1))
                    return r
                OP("tensor", [bcdw_sb, xcb], [p_bcd], bcdg)
                yield
                OP("scalar", [p_bcd], [bcd],
                   lambda e, bcd=bcd: e.activation(
                       out=bcd[:, :], in_=p_bcd[0:96, :], func=AF.Identity))
                DMA("sync", cc_bcd_in, bcd[:], [bcd], ["cc_bcd_in"])
                OP("gpsimd", ["cc_bcd_in", "cc_chain"],
                   ["cc_bcd_out", "cc_chain"],
                   lambda e: e.collective_compute(
                       "AllGather", ALU.bypass,
                       replica_groups=[[0, 1, 2, 3], [4, 5, 6, 7]],
                       ins=[cc_bcd_in], outs=[cc_bcd_out]), cc=True)
                mk_mm(4)
                mk_mm(5)
                yield
                mk_evac(4)
                mk_evac(5)
                mk_mm(6)
                mk_mm(7)
                yield
                mk_evac(6)
                mk_evac(7)
                yield

            def emitGather(ci):
                bcd = bcd2[ci % 2]
                dlr_bf = dlr2[ci % 2]
                bcs = bc_scr2[ci % 2]
                bcs_nm = f"bc_scr{ci % 2}"
                DMA("sync", bcd[:], cc_bcd_out[0:96, :],
                    ["cc_bcd_out"], [bcd])
                for r in range(1, 4):
                    OP("gpsimd", ["cc_bcd_out", bcd], [bcd],
                       lambda e, r=r, bcd=bcd: e.dma_start(
                           out=bcd[:], in_=cc_bcd_out[r * 96:(r + 1) * 96, :],
                           accum_op=ALU.add), dma=True)
                OP("gpsimd", [bcd], [dlr_bf],
                   lambda e, bcd=bcd, dlr_bf=dlr_bf: e.tensor_copy(
                       dlr_bf[:], bcd[0:DR, :]))
                DMA("sync", bcs[:, 0:LC], bcd[DR:DR + 16, :],
                    [bcd], [bcs_nm])
                DMA("sync", bcs[:, LC:2 * LC], bcd[DR + 16:96, :],
                    [bcd], [bcs_nm])
                if debug and ci == 0:
                    DMA("sync", dbg["d_bcd"], bcd[:], [bcd], ["d_bcd"])

            def emitDup(ci, m):
                dlr_bf = dlr2[ci % 2]
                delta_m = delta2[ci % 2][m]
                pm = p_mm[2]
                OP("tensor", [dupw_sb, dlr_bf], [pm],
                   lambda e, m=m, pm=pm, dlr_bf=dlr_bf: e.matmul(
                       pm[:], dupw_sb[:, m * 128:(m + 1) * 128],
                       dlr_bf[:, :], start=True, stop=True))
                OP("scalar", [pm, dupb_sb], [sp_t1],
                   lambda e, m=m, pm=pm: e.activation(
                       out=sp_t1[:], in_=pm[:], func=AF.Abs,
                       bias=dupb_sb[:, m:m + 1]))
                OP("scalar", [sp_t1], [sp_t1],
                   lambda e: e.activation(out=sp_t1[:], in_=sp_t1[:],
                                          func=AF.Exp, scale=-1.0))
                OP("scalar", [sp_t1], [sp_t1],
                   lambda e: e.activation(out=sp_t1[:], in_=sp_t1[:],
                                          func=AF.Ln, bias=1.0))
                OP("scalar", [pm, dupb_sb], [sp_t2],
                   lambda e, m=m, pm=pm: e.activation(
                       out=sp_t2[:], in_=pm[:], func=AF.Relu,
                       bias=dupb_sb[:, m:m + 1]))
                OP(BISECT1, [sp_t1, sp_t2], [delta_m],
                   lambda e, delta_m=delta_m: e.tensor_tensor(
                       out=delta_m[:], in0=sp_t1[:], in1=sp_t2[:],
                       op=ALU.add))
                if debug and ci == 0:
                    DMA("sync", dbg["d_delta"][:, m * LC:(m + 1) * LC],
                        delta_m[:], [delta_m], ["d_delta"])

            def emitUY(ci, m):
                xc = xc2[ci % 2]
                delta_m = delta2[ci % 2][m]
                OP(BISECT1, [delta_m, xc], [u_dc[m]],
                   lambda e, m=m, xc=xc, delta_m=delta_m: e.tensor_tensor(
                       out=u_dc[m][:], in0=delta_m[:],
                       in1=xc[:, m, :], op=ALU.mult))
                OP("vector", [xc, dparam_sb], [y_dc[m]],
                   lambda e, m=m, xc=xc: e.tensor_scalar(
                       out=y_dc[m][:], in0=xc[:, m, :],
                       scalar1=dparam_sb[:, m:m + 1], scalar2=None,
                       op0=ALU.mult))

            def emitBcast(ci, n):
                bcs = bc_scr2[ci % 2]
                bcs_nm = f"bc_scr{ci % 2}"
                bmcm = bmcm2[n % 2]
                OP("sync", [bcs_nm, "bc_chain"], [bmcm, "bc_chain"],
                   lambda e, bcs=bcs, n=n, bmcm=bmcm: e.dma_start(
                       out=bmcm[:],
                       in_=bcs[n:n + 1, :].partition_broadcast(128)),
                   dma=True)

            def emitScanGroup(ci, n):
                bmcm = bmcm2[n % 2]
                bm = bmcm[:, 0, :]
                cm = bmcm[:, 1, :]
                delta_c = delta2[ci % 2]
                for dc in range(NDC):
                    i2 = (n * NDC + dc) % 2
                    ab, bx, hh, ww = abuf[i2], bxbuf[i2], hbuf[i2], wwbuf[i2]
                    col = dc * NST + n
                    OP("scalar", [delta_c[dc], A_sb], [ab],
                       lambda e, dc=dc, col=col, ab=ab, delta_c=delta_c:
                       e.activation(
                           out=ab[:], in_=delta_c[dc][:], func=AF.Exp,
                           scale=A_sb[:, col:col + 1]))
                    OP("gpsimd", [u_dc[dc], bmcm], [bx],
                       lambda e, dc=dc, bm=bm, bx=bx: e.tensor_tensor(
                           out=bx[:], in0=u_dc[dc][:], in1=bm[:],
                           op=ALU.mult))
                    p.op("vector",
                         lambda e, ab=ab, bx=bx, hh=hh, col=col:
                         e.tensor_tensor_scan(
                             out=hh[:], data0=ab[:], data1=bx[:],
                             initial=state[:, col:col + 1],
                             op0=ALU.mult, op1=ALU.add),
                         reads=[ab, bx, state], writes=[hh],
                         fifo_unsafe=True)
                    OP("gpsimd", [hh], [state],
                       lambda e, hh=hh, col=col: e.tensor_copy(
                           state[:, col:col + 1], hh[:, LC - 1:LC]))
                    ymul_eng = "gpsimd" if dc % 2 == 1 else "vector"
                    OP(ymul_eng, [hh, cm], [ww],
                       lambda e, hh=hh, cm=cm, ww=ww: e.tensor_tensor(
                           out=ww[:], in0=hh[:], in1=cm[:], op=ALU.mult))
                    yadd_eng = "gpsimd" if dc % 2 == 0 else "vector"
                    OP(yadd_eng, [ww, y_dc[dc]], [y_dc[dc]],
                       lambda e, ww=ww, dc=dc: e.tensor_tensor(
                           out=y_dc[dc][:], in0=y_dc[dc][:],
                           in1=ww[:], op=ALU.add))

            def emitChunkTail(ci):
                sres = sres2[ci % 2]
                ccyi, ccyo = cc_y_in2[ci % 2], cc_y_out2[ci % 2]
                ccyi_nm, ccyo_nm = f"cc_y_in{ci % 2}", f"cc_y_out{ci % 2}"
                if debug and ci == 0:
                    for m in range(NDC):
                        DMA("sync", dbg["d_yacc"][:, m * LC:(m + 1) * LC],
                            y_dc[m][:], [y_dc[m]], ["d_yacc"])
                for dc in range(NDC):
                    OP("vector", [y_dc[dc], sres], [ybar_t],
                       lambda e, dc=dc, sres=sres: e.tensor_tensor(
                           out=ybar_t[:], in0=y_dc[dc][:],
                           in1=sres[:, dc, :], op=ALU.mult))
                    DMA("sync", ccyi[dc * 128:(dc + 1) * 128, :], ybar_t[:],
                        [ybar_t], [ccyi_nm])
                    if ci + 1 < EMIT_CHUNKS:
                        emitUY(ci + 1, dc)
                OP("gpsimd", [ccyi_nm, "cc_chain"], [ccyo_nm, "cc_chain"],
                   lambda e, ccyi=ccyi, ccyo=ccyo: e.collective_compute(
                       "AllGather", ALU.bypass,
                       replica_groups=[[0, 1, 2, 3], [4, 5, 6, 7]],
                       ins=[ccyi], outs=[ccyo]), cc=True)

            def genPost(ci):
                ccyo = cc_y_out2[ci % 2]
                ccyo_nm = f"cc_y_out{ci % 2}"
                OP("sync", [ccyo_nm], [yall],
                   lambda e, ccyo=ccyo: e.dma_start(
                       out=yall[:],
                       in_=ccyo[:, bass.ds(snapped["qv"], QT)].rearrange(
                           "(k p) t -> p k t", p=128)), dma=True)
                DMA("sync", x_qt[:, 0, :], x_q[ci], ["x_q"], [x_qt])
                if debug and ci == 0:
                    DMA("sync", dbg["d_yall"],
                        yall[:].rearrange("p a l -> p (a l)"), [yall], ["d_yall"])
                yield
                pz = [p_mm[3], p_mm[4]]

                def opg(e, nh):
                    r = None
                    for k in range(16):
                        r = e.matmul(
                            pz[nh][:], yall[:, k, :],
                            wout_t[:, k, nh * 512:(nh + 1) * 512],
                            start=(k == 0), stop=(k == 15))
                    return r
                OP("tensor", [yall, wout_t], [pz[0]],
                   lambda e: opg(e, 0))
                yield
                OP("tensor", [yall, wout_t], [pz[1]],
                   lambda e: opg(e, 1))
                yield
                for nh in range(2):
                    OP("vector", [x_qt, pz[nh]], [zb],
                       lambda e, nh=nh, pz=pz: e.scalar_tensor_tensor(
                           out=zb[:, 0, nh * 512:(nh + 1) * 512],
                           in0=x_qt[:, 0, nh * 512:(nh + 1) * 512],
                           scalar=2.0, in1=pz[nh][:], op0=ALU.mult,
                           op1=ALU.add))
                OP("scalar", [zb], [s2, dumpb],
                   lambda e: e.activation(
                       out=dumpb[:], in_=zb[:, 0], func=AF.Square,
                       accum_out=s2[:, 0:1]))
                yield
                OP("vector", [zb], [s1],
                   lambda e: e.tensor_reduce(
                       out=s1[:, 0:1], in_=zb[:, 0], axis=AXL.X, op=ALU.add))
                OP("vector", [s1], [meanb],
                   lambda e: e.tensor_scalar(out=meanb[:, 0:1], in0=s1[:, 0:1],
                                             scalar1=1.0 / D, scalar2=None,
                                             op0=ALU.mult))
                OP("vector", [meanb], [varb],
                   lambda e: e.tensor_tensor(out=varb[:, 0:1], in0=meanb[:, 0:1],
                                             in1=meanb[:, 0:1], op=ALU.mult))
                OP("vector", [s2, varb], [varb],
                   lambda e: e.scalar_tensor_tensor(
                       out=varb[:, 0:1], in0=s2[:, 0:1], scalar=1.0 / D,
                       in1=varb[:, 0:1], op0=ALU.mult, op1=ALU.subtract))
                OP("scalar", [varb], [rstd1],
                   lambda e: e.activation(out=rstd1[:, 0:1], in_=varb[:, 0:1],
                                          func=AF.Ln, bias=EPS))
                OP("scalar", [rstd1], [rstd1],
                   lambda e: e.activation(out=rstd1[:, 0:1], in_=rstd1[:, 0:1],
                                          func=AF.Exp, scale=-0.5))
                yield
                OP("vector", [zb, meanb, rstd1], [zb],
                   lambda e: e.tensor_scalar(
                       out=zb[:, 0], in0=zb[:, 0],
                       scalar1=meanb[:, 0:1], scalar2=rstd1[:, 0:1],
                       op0=ALU.subtract, op1=ALU.mult))
                OP("scalar", [zb], [x1bb],
                   lambda e: e.activation(out=x1bb[:, 0], in_=zb[:, 0],
                                          func=AF.Identity))
                if debug and ci == 0:
                    DMA("sync", dbg["d_x1"], zb[:, 0, :], [zb], ["d_x1"])
                DMA("sync", x1_spill[ci * QT:(ci + 1) * QT, :], zb[:, 0, :],
                    [zb], ["x1_spill"])
                DMA("sync", x1b_spill[ci * QT:(ci + 1) * QT, :], x1bb[:, 0, :],
                    [x1bb], ["x1b_spill"])
                yield

            def drain(g, k=1):
                if g is not None:
                    for _ in range(k):
                        if next(g, "END") == "END":
                            return

            drain(genA(0), 99)
            emitGather(0)
            for m in range(NDC):
                emitDup(0, m)
                emitUY(0, m)
            emitBcast(0, 0)
            emitBcast(0, 1)
            slotA = {0: 1, 1: 1, 2: 1, 3: 1, 4: 1, 5: 1, 6: 1, 7: 1,
                     8: 1, 9: 1, 10: 1, 11: 1, 12: 1, 13: 1}
            slotP = {2: 1, 8: 1, 9: 1, 10: 1, 11: 1, 12: 1}
            gP = None
            for ci in range(EMIT_CHUNKS):
                gA = genA(ci + 1) if ci + 1 < EMIT_CHUNKS else None
                for n in range(NST):
                    drain(gP, slotP.get(n, 0))
                    drain(gA, slotA.get(n, 0))
                    if ci + 1 < EMIT_CHUNKS:
                        if n == 13:
                            emitGather(ci + 1)
                        elif n == 14:
                            emitDup(ci + 1, 0)
                            emitDup(ci + 1, 1)
                        elif n == 15:
                            emitDup(ci + 1, 2)
                            emitDup(ci + 1, 3)
                    emitScanGroup(ci, n)
                    if n + 2 < NST:
                        emitBcast(ci, n + 2)
                    elif ci + 1 < EMIT_CHUNKS:
                        emitBcast(ci + 1, n + 2 - NST)
                emitChunkTail(ci)
                drain(gP, 99)
                gP = genPost(ci)
            drain(gP, 99)

        p.close_fence(nmM)
        if EMIT_FFN:
            with ExitStack() as esF:
                nmF = []
                sbF = tracked(esF, nmF)
                gbuf = sbF("gbuf", [128, 32, FSUB], BF16)
                t1_sb = sbF("t1_sb", [128, 512])
                ztmp = sbF("ztmp", [128, 512])
                out_t = sbF("out_t", [128, D])
                for sub in range(NFS):
                    R0 = sub * FSUB
                    if sub > 0:
                        p.close_fence([n + str(sub - 1) for n in
                                       ("w3rhs", "z2all", "x1row0", "x1row1")])
                    es123 = ExitStack()
                    nm123 = []
                    _sb123 = tracked(es123, nm123)
                    sb123 = lambda nme, shape, dt=BF16: _sb123(
                        nme + str(sub), shape, dt)
                    hobuf = sb123("hobuf", [128, 32, FSUB])
                    x1_fm = sb123("x1_fm", [128, 8, FSUB])
                    wblk = [sb123(f"wblk{i}", [128, 8, 128]) for i in range(4)]
                    with ExitStack() as esX:
                        x1tmb = esX.enter_context(
                            nc.sbuf_tensor(f"x1tmb{sub}", [128, 8, D], BF16))
                        DMA("sync", x1tmb[:],
                            x1b_spill[R0:R0 + FSUB, :].rearrange(
                                "(j p) d -> p j d", p=128), ["x1b_spill"], [x1tmb])
                        for dc8 in range(8):
                            for j in range(8):
                                pt = p_tr[(dc8 * 8 + j) % 2]
                                OP("tensor", [x1tmb, ident], [pt],
                                   lambda e, j=j, dc8=dc8, pt=pt: e.transpose(
                                       pt[:, 0:128], x1tmb[:, j, dc8 * 128:(dc8 + 1) * 128],
                                       ident[:]))
                                OP("scalar", [pt], [x1_fm],
                                   lambda e, j=j, dc8=dc8, pt=pt: e.activation(
                                       out=x1_fm[:, dc8, j * 128:(j + 1) * 128],
                                       in_=pt[:, 0:128], func=AF.Identity))
                    p.close_fence([f"x1tmb{sub}"])
                    for m in range(32):
                        wb1, wb2 = wblk[2 * (m % 2)], wblk[2 * (m % 2) + 1]
                        DMA("gpsimd", wb1[:],
                            w1t[:, m * 128:(m + 1) * 128].rearrange(
                                "(k p) m -> p k m", p=128), ["w1t"], [wb1])
                        DMA("gpsimd", wb2[:],
                            w2t[:, m * 128:(m + 1) * 128].rearrange(
                                "(k p) m -> p k m", p=128), ["w2t"], [wb2])
                        for tt in range(2):
                            pa, pb = p_mm[tt], p_mm[2 + tt]

                            def g1(e, wb1=wb1, tt=tt, pa=pa):
                                r = None
                                for k in range(8):
                                    r = e.matmul(
                                        pa[:], wb1[:, k, :],
                                        x1_fm[:, k, tt * 512:(tt + 1) * 512],
                                        start=(k == 0), stop=(k == 7))
                                return r

                            def g2(e, wb2=wb2, tt=tt, pb=pb):
                                r = None
                                for k in range(8):
                                    r = e.matmul(
                                        pb[:], wb2[:, k, :],
                                        x1_fm[:, k, tt * 512:(tt + 1) * 512],
                                        start=(k == 0), stop=(k == 7))
                                return r
                            OP("tensor", [wb1, x1_fm], [pa], g1)
                            OP("tensor", [wb2, x1_fm], [pb], g2)
                            OP("scalar", [pa, w1s_sb], [t1_sb],
                               lambda e, m=m, pa=pa: e.activation(
                                   out=t1_sb[:], in_=pa[:], func=AF.Silu,
                                   scale=w1s_sb[:, m:m + 1]))
                            OP("vector", [pb, w2s_sb, t1_sb], [gbuf],
                               lambda e, m=m, tt=tt, pb=pb: e.scalar_tensor_tensor(
                                   out=gbuf[:, m, tt * 512:(tt + 1) * 512],
                                   in0=pb[:], scalar=w2s_sb[:, m:m + 1],
                                   in1=t1_sb[:], op0=ALU.mult, op1=ALU.mult))
                    if debug and sub == 0:
                        DMA("sync", dbg["d_x1s"], x1b_spill, ["x1b_spill"], ["d_x1s"])
                        DMA("sync", dbg["d_g"],
                            gbuf[:].rearrange("p a b -> p (a b)"), [gbuf], ["d_g"])
                    for st_i, (wt_ap, wt_name, ssb, src, dst) in enumerate([
                            (h1t, "h1t", h1s_sb, gbuf, hobuf),
                            (h2t, "h2t", h2s_sb, hobuf, gbuf)]):
                        with ExitStack() as esH:
                            hblk = [esH.enter_context(nc.sbuf_tensor(
                                f"hblk{sub}_{st_i}_{i}", [128, 32, 128], BF16))
                                for i in range(2)]
                            for m in range(32):
                                hb = hblk[m % 2]
                                DMA("gpsimd", hb[:],
                                    wt_ap[:, m * 128:(m + 1) * 128].rearrange(
                                        "(k p) m -> p k m", p=128), [wt_name], [hb])
                                for tt in range(2):
                                    pa = p_mm[tt]

                                    def hg(e, hb=hb, tt=tt, pa=pa, src=src):
                                        r = None
                                        for k in range(32):
                                            r = e.matmul(
                                                pa[:], hb[:, k, :],
                                                src[:, k, tt * 512:(tt + 1) * 512],
                                                start=(k == 0), stop=(k == 31))
                                        return r
                                    OP("tensor", [hb, src], [pa], hg)
                                    OP("scalar", [pa, ssb], [dst],
                                       lambda e, m=m, tt=tt, pa=pa, ssb=ssb,
                                       dst=dst: e.activation(
                                           out=dst[:, m, tt * 512:(tt + 1) * 512],
                                           in_=pa[:], func=AF.Gelu,
                                           scale=ssb[:, m:m + 1]))
                        p.close_fence([f"hblk{sub}_{st_i}_{i}"
                                       for i in range(2)])
                    if debug and sub == 0:
                        DMA("sync", dbg["d_h1"],
                            hobuf[:].rearrange("p a b -> p (a b)"), [hobuf], ["d_h1"])
                        DMA("sync", dbg["d_h2"],
                            gbuf[:].rearrange("p a b -> p (a b)"), [gbuf], ["d_h2"])
                    es123.close()
                    p.close_fence(nm123)
                    with ExitStack() as esW:
                        nmW = []
                        _sbW = tracked(esW, nmW)
                        sbW = lambda nme, shape, dt=F32: _sbW(
                            nme + str(sub), shape, dt)
                        w3rhs = sbW("w3rhs", [128, 32, 512], BF16)
                        z2all = sbW("z2all", [128, 8, D])
                        x1row = [sbW(f"x1row{i}", [128, D]) for i in range(2)]
                        for nh in range(2):
                            DMA("gpsimd", w3rhs[:],
                                w3t[:, nh * 512:(nh + 1) * 512].rearrange(
                                    "(k p) m -> p k m", p=128), ["w3t"], [w3rhs])
                            for tc in range(8):
                                pzn = p_mm[tc % 2]

                                def w3g(e, tc=tc, pzn=pzn):
                                    r = None
                                    for k in range(32):
                                        r = e.matmul(
                                            pzn[:],
                                            gbuf[:, k, tc * 128:(tc + 1) * 128],
                                            w3rhs[:, k, :],
                                            start=(k == 0), stop=(k == 31))
                                    return r
                                OP("tensor", [gbuf, w3rhs], [pzn], w3g)
                                xr = x1row[tc % 2]
                                OP("sync", ["x1_spill"], [xr],
                                   lambda e, tc=tc, R0=R0, xr=xr: e.dma_start(
                                       out=xr[:],
                                       in_=x1_spill[R0 + tc * 128:
                                                    R0 + (tc + 1) * 128, :]),
                                   dma=True)
                                OP("vector", [pzn, w3sRep], [ztmp],
                                   lambda e, nh=nh, pzn=pzn: e.tensor_tensor(
                                       out=ztmp[:], in0=pzn[:],
                                       in1=w3sRep[:, nh * 512:(nh + 1) * 512],
                                       op=ALU.mult))
                                OP("vector", [ztmp, xr], [z2all],
                                   lambda e, tc=tc, nh=nh, xr=xr: e.tensor_tensor(
                                       out=z2all[:, tc, nh * 512:(nh + 1) * 512],
                                       in0=ztmp[:],
                                       in1=xr[:, nh * 512:(nh + 1) * 512],
                                       op=ALU.add))
                        if debug and sub == 0:
                            DMA("sync", dbg["d_z2"],
                                z2all[:].rearrange("p a b -> p (a b)"),
                                [z2all], ["d_z2"])
                        for tc in range(8):
                            OP("scalar", [z2all], [s2, dumpb],
                               lambda e, tc=tc: e.activation(
                                   out=dumpb[:], in_=z2all[:, tc], func=AF.Square,
                                   accum_out=s2[:, tc:tc + 1]))
                            OP("vector", [z2all], [s1],
                               lambda e, tc=tc: e.tensor_reduce(
                                   out=s1[:, tc:tc + 1], in_=z2all[:, tc],
                                   axis=AXL.X, op=ALU.add))
                        OP("vector", [s1], [meanb],
                           lambda e: e.tensor_scalar(
                               out=meanb[:], in0=s1[:], scalar1=1.0 / D,
                               scalar2=None, op0=ALU.mult))
                        OP("vector", [meanb], [varb],
                           lambda e: e.tensor_tensor(
                               out=varb[:], in0=meanb[:], in1=meanb[:],
                               op=ALU.mult))
                        OP("vector", [s2, varb], [varb],
                           lambda e: e.scalar_tensor_tensor(
                               out=varb[:], in0=s2[:], scalar=1.0 / D,
                               in1=varb[:], op0=ALU.mult, op1=ALU.subtract))
                        OP("scalar", [varb], [rstd1],
                           lambda e: e.activation(out=rstd1[:], in_=varb[:],
                                                  func=AF.Ln, bias=EPS))
                        OP("scalar", [rstd1], [rstd1],
                           lambda e: e.activation(out=rstd1[:], in_=rstd1[:],
                                                  func=AF.Exp, scale=-0.5))
                        if debug:
                            DMA("sync", dbg["d_st"],
                                s1[:], [s1], ["d_st"])
                            DMA("sync", dbg["d_st2"],
                                s2[:], [s2], ["d_st2"])
                            DMA("sync", dbg["d_rs"],
                                rstd1[:], [rstd1], ["d_rs"])
                        for tc in range(8):
                            OP("vector", [z2all, meanb, rstd1], [out_t],
                               lambda e, tc=tc: e.tensor_scalar(
                                   out=out_t[:], in0=z2all[:, tc],
                                   scalar1=meanb[:, tc:tc + 1],
                                   scalar2=rstd1[:, tc:tc + 1],
                                   op0=ALU.subtract, op1=ALU.mult))
                            DMA("sync", out[R0 + tc * 128:R0 + (tc + 1) * 128, :],
                                out_t[:], [out_t], ["out"])

        p.fence()
        snap = dict(p.cum)
        for e in ENGS:
            p.op(e, lambda en: en.nop(), reads=(), writes=())

        streams = p.finalize()

        def emit(engname):
            def f(e):
                for fn, inc, ch, waits in streams[engname]:
                    for (sch, sv) in waits:
                        e.wait_ge(sems[sch], sv)
                    instr = fn(e)
                    if instr is not None:
                        instr.then_inc(sems[ch], inc)
            return f

        block.sync(emit("sync"))
        block.scalar(emit("scalar"))
        block.vector(emit("vector"))
        block.tensor(emit("tensor"))
        block.gpsimd(emit("gpsimd"))

    return nc


def _stage_inputs(inputs, c):
    g, q = c // 4, c % 4
    d0 = q * DSL
    asc = np.ascontiguousarray
    x = np.asarray(inputs["L0_attn_x"])
    d = {}
    d["x_seq"] = asc(x[g])
    d["x_q"] = asc(x[g].reshape(NCHUNK, 4, QT, D)[:, q])
    wi_full = np.asarray(inputs["L0_attn_in_proj_w"])
    d["wi"] = asc(np.concatenate(
        [wi_full[:, d0:d0 + DSL], wi_full[:, DI + d0:DI + d0 + DSL]], axis=1))
    cw = np.asarray(inputs["L0_attn_conv_w"])[d0:d0 + DSL]
    d["convw"] = asc(cw.reshape(NDC, 128, KC).transpose(1, 0, 2).reshape(128, NDC * KC))
    cb = np.asarray(inputs["L0_attn_conv_b"])[d0:d0 + DSL]
    d["convb"] = asc(cb.reshape(NDC, 128).T)
    d["bcdw"] = asc(np.asarray(inputs["L0_attn_bcd_w"])[d0:d0 + DSL])
    d["dupw"] = asc(np.asarray(inputs["L0_attn_dup_w"])[:, d0:d0 + DSL])
    db = np.asarray(inputs["L0_attn_dup_b"])[d0:d0 + DSL]
    d["dupb"] = asc(db.reshape(NDC, 128).T)
    lm = np.asarray(inputs["L0_attn_log_minus_A"])[d0:d0 + DSL]
    d["lmA"] = asc(lm.reshape(NDC, 128, NST).transpose(1, 0, 2).reshape(128, NDC * NST))
    dp = np.asarray(inputs["L0_attn_D_param"])[d0:d0 + DSL]
    d["dparam"] = asc(dp.reshape(NDC, 128).T)
    rw = np.asarray(inputs["L0_attn_rms_w"])
    d["rmsw"] = asc(rw.reshape(8, 128).T)
    d["wout"] = asc(np.asarray(inputs["L0_attn_out_proj_w"]))
    d["w1t"] = asc(np.asarray(inputs["L0_ffn_w1_w"]).T)
    d["w2t"] = asc(np.asarray(inputs["L0_ffn_w2_w"]).T)
    d["h1t"] = asc(np.asarray(inputs["L0_ffn_h1_w"]).T)
    d["h2t"] = asc(np.asarray(inputs["L0_ffn_h2_w"]).T)
    d["w3t"] = asc(np.asarray(inputs["L0_ffn_w3_w"]).T)
    for nme, key in [("w1s", "L0_ffn_w1_s"), ("w2s", "L0_ffn_w2_s"),
                     ("h1s", "L0_ffn_h1_s"), ("h2s", "L0_ffn_h2_s")]:
        s = np.asarray(inputs[key])
        d[nme] = asc(s.reshape(32, 128).T)
    d["w3s"] = asc(np.asarray(inputs["L0_ffn_w3_s"]).reshape(1, D))
    d["ident_in"] = np.eye(128, dtype=np.float32)
    return d


def kernel(**inputs):
    nc = build(debug=DEBUG)
    in_maps = [_stage_inputs(inputs, c) for c in range(8)]
    res = run_bass_kernel_spmd(nc, in_maps, core_ids=list(range(8)))
    full = np.zeros((B, L, D), np.float32)
    for c in range(8):
        g, q = c // 4, c % 4
        oc = np.asarray(res.results[c]["out"])
        oc = oc.reshape(NCHUNK, QT, D)
        for ci in range(NCHUNK):
            full[g, ci * LC + q * QT:ci * LC + (q + 1) * QT, :] = oc[ci]
    if DEBUG:
        kernel.last_results = res
    return full


# version 19
# speedup vs baseline: 1.1230x; 1.1230x over previous
BISECT1 = "gpsimd"
"""Fused single-launch megakernel for mk_seq_postln_d1 (Mamba block + ternary
FFN, depth 1, x: [2, 8192, 1024]) on 8 trn2 NeuronCores.

Parallelization
  - 2 sequences x 4-core groups. Within a group each core owns a 512-wide
    slice of the DI=2048 inner dim for the whole mamba part (in_proj slice,
    conv, selective scan) over the full 8192-token sequence, so the scan has
    no cross-core sequential chain.
  - Each core owns the q-th 256-token quarter of every 1024-token chunk for
    out_proj / layernorms / FFN (2048 tokens per core).
  - Cross-core data: two AllGather collectives per chunk inside each 4-core
    group: bcd partials [96 x 1024] fp32 and ybar slices [512 x 1024] bf16.
Layouts
  - mamba inner tensors feature-major [d on partitions x tokens free]; scan
    via tensor_tensor_scan along free dim with per-(d-chunk, n) state column.
  - residual / LN / FFN-output pipeline token-major; FFN hidden feature-major.
  - bf16 matmuls with fp32 PSUM accumulation, fp32 elementwise + scan state.
"""

import numpy as np
import concourse.bass as bass
import concourse.mybir as mybir
from concourse.bass_utils import run_bass_kernel_spmd
from contextlib import ExitStack

F32 = mybir.dt.float32
BF16 = mybir.dt.bfloat16
I32 = mybir.dt.int32
AF = mybir.ActivationFunctionType
ALU = mybir.AluOpType
AXL = mybir.AxisListType
ET = mybir.EngineType

B, L, D = 2, 8192, 1024
DI, NST, DR, KC = 2048, 16, 64, 4
DH = 4096
EPS = 1e-5
LC = 512
NCHUNK = L // LC
NSUB = LC // 512
DSL = 512
NDC = DSL // 128
QT = LC // 4
NTOK = NCHUNK * QT
FSUB = 1024
NFS = NTOK // FSUB

ENGS = ("sync", "scalar", "vector", "tensor", "gpsimd")

EMIT_CHUNKS = NCHUNK
DBG_CI = 1
EMIT_FFN = True
DEBUG = False
SERIAL = False


PSLOT = {"sync": 6, "scalar": 1, "gpsimd": 6}
DMAQ = ("sync", "scalar", "gpsimd")
CHANS = tuple((e, "c") for e in ENGS) + tuple(
    (e, "d", i) for e in DMAQ for i in range(max(6, *PSLOT.values()))) + (
    ("gpsimd", "cc"),)


class Prog:

    def __init__(self):
        self.ops = []
        self.cum = {ch: 0 for ch in CHANS}
        self.last_w = {}
        self.readers = {}
        self.fence_lv = {ch: 0 for ch in CHANS}
        self.dma_i = {e: 0 for e in DMAQ}
        self.unsafe = set()

    @staticmethod
    def _name(b):
        if isinstance(b, str):
            return b
        n = getattr(b, "name", None)
        if n is not None:
            return n
        return b.tensor.name

    def op(self, eng, fn, reads=(), writes=(), dma=False, fifo_unsafe=False,
           cc=False):
        if dma:
            i = self.dma_i[eng]
            self.dma_i[eng] = i + 1
            ch = (eng, "d", i % PSLOT[eng])
            inc = 16
        elif cc:
            ch = (eng, "cc")
            inc = 1
        else:
            ch = (eng, "c")
            inc = 1
        waits = {}

        def add(sch, sv):
            if sv <= 0:
                return
            if sch == (eng, "c") and (sch, sv) not in self.unsafe:
                return
            waits[sch] = max(waits.get(sch, 0), sv)

        for b in reads:
            lw = self.last_w.get(self._name(b))
            if lw:
                add(*lw)
        for b in writes:
            nm = self._name(b)
            lw = self.last_w.get(nm)
            if lw:
                add(*lw)
            for r in self.readers.get(nm, ()):
                add(*r)
        for sch, lv in self.fence_lv.items():
            add(sch, lv)
        if dma:
            waits[ch] = max(waits.get(ch, 0), self.cum[ch])
        self.cum[ch] += inc
        me = (ch, self.cum[ch])
        if not dma and eng in ("vector", "gpsimd"):
            self.unsafe.add(me)
        for b in reads:
            self.readers.setdefault(self._name(b), []).append(me)
        for b in writes:
            nm = self._name(b)
            self.last_w[nm] = me
            self.readers[nm] = []
        self.ops.append((eng, fn, inc, ch, waits))

    def fence(self):
        self.fence_lv = dict(self.cum)

    def close_fence(self, names):
        for nm in names:
            for t in ([self.last_w[nm]] if nm in self.last_w else []) +                      list(self.readers.get(nm, ())):
                ch, cum = t
                if self.fence_lv.get(ch, 0) < cum:
                    self.fence_lv[ch] = cum

    def finalize(self):
        covered = {e: {ch: 0 for ch in CHANS} for e in ENGS}
        streams = {e: [] for e in ENGS}
        for eng, fn, inc, ch, waits in self.ops:
            w2 = []
            for sch, sv in sorted(waits.items(), key=lambda kv: str(kv[0])):
                if sv > covered[eng][sch]:
                    w2.append((sch, sv))
                    covered[eng][sch] = sv
            streams[eng].append((fn, inc, ch, w2))
        return streams


def build(debug=False):
    nc = bass.Bass("TRN2", target_bir_lowering=False)

    xin = nc.dram_tensor("x_seq", [L, D], F32, kind="ExternalInput").ap()
    x_q = nc.dram_tensor("x_q", [NCHUNK, QT, D], F32, kind="ExternalInput").ap()
    wi = nc.dram_tensor("wi", [D, 1024], F32, kind="ExternalInput").ap()
    convw = nc.dram_tensor("convw", [128, NDC * KC], F32, kind="ExternalInput").ap()
    convb = nc.dram_tensor("convb", [128, NDC], F32, kind="ExternalInput").ap()
    bcdw = nc.dram_tensor("bcdw", [DSL, 96], F32, kind="ExternalInput").ap()
    dupw = nc.dram_tensor("dupw", [DR, DSL], F32, kind="ExternalInput").ap()
    dupb = nc.dram_tensor("dupb", [128, NDC], F32, kind="ExternalInput").ap()
    lmA = nc.dram_tensor("lmA", [128, NDC * NST], F32, kind="ExternalInput").ap()
    dparam = nc.dram_tensor("dparam", [128, NDC], F32, kind="ExternalInput").ap()
    rmsw = nc.dram_tensor("rmsw", [128, 8], F32, kind="ExternalInput").ap()
    wout = nc.dram_tensor("wout", [DI, D], F32, kind="ExternalInput").ap()
    w1t = nc.dram_tensor("w1t", [D, DH], F32, kind="ExternalInput").ap()
    w2t = nc.dram_tensor("w2t", [D, DH], F32, kind="ExternalInput").ap()
    h1t = nc.dram_tensor("h1t", [DH, DH], F32, kind="ExternalInput").ap()
    h2t = nc.dram_tensor("h2t", [DH, DH], F32, kind="ExternalInput").ap()
    w3t = nc.dram_tensor("w3t", [DH, D], F32, kind="ExternalInput").ap()
    w1s = nc.dram_tensor("w1s", [128, 32], F32, kind="ExternalInput").ap()
    w2s = nc.dram_tensor("w2s", [128, 32], F32, kind="ExternalInput").ap()
    h1s = nc.dram_tensor("h1s", [128, 32], F32, kind="ExternalInput").ap()
    h2s = nc.dram_tensor("h2s", [128, 32], F32, kind="ExternalInput").ap()
    w3s = nc.dram_tensor("w3s", [1, D], F32, kind="ExternalInput").ap()
    ident_in = nc.dram_tensor("ident_in", [128, 128], F32, kind="ExternalInput").ap()
    out = nc.dram_tensor("out", [NTOK, D], F32, kind="ExternalOutput").ap()

    cc_bcd_in = nc.dram_tensor("cc_bcd_in", [96, LC], F32, kind="Internal").ap()
    cc_bcd_out = nc.dram_tensor("cc_bcd_out", [4 * 96, LC], F32, kind="Internal").ap()
    cc_y_in2 = [nc.dram_tensor(f"cc_y_in{j}", [DSL, LC], BF16,
                               kind="Internal").ap() for j in range(2)]
    cc_y_out2 = [nc.dram_tensor(f"cc_y_out{j}", [DI, LC], BF16,
                                kind="Internal").ap() for j in range(2)]
    bc_scr2 = [nc.dram_tensor(f"bc_scr{j}", [16, 2 * LC], F32,
                               kind="Internal").ap() for j in range(2)]
    x1_spill = nc.dram_tensor("x1_spill", [NTOK, D], F32, kind="Internal").ap()
    x1b_spill = nc.dram_tensor("x1b_spill", [NTOK, D], BF16, kind="Internal").ap()

    dbg = {}
    if debug:
        for nme, shape, dt in [
            ("d_xn", [128, 8 * 512], BF16),
            ("d_xc", [128, NDC * LC], F32),
            ("d_bcd", [96, LC], F32),
            ("d_delta", [128, NDC * LC], F32),
            ("d_yacc", [128, NDC * LC], F32),
            ("d_x1", [QT, D], F32),
            ("d_yall", [128, 16 * QT], BF16),
            ("d_x1s", [NTOK, D], BF16),
            ("d_g", [128, 32 * FSUB], BF16),
            ("d_h1", [128, 32 * FSUB], BF16),
            ("d_h2", [128, 32 * FSUB], BF16),
            ("d_z2", [128, 8 * D], F32),
            ("d_st", [128, 8], F32),
            ("d_st2", [128, 8], F32),
            ("d_rs", [128, 8], F32),
            ("d_state", [128, NDC * NST], F32),
            ("d_zb", [128, 2 * D], F32),
        ]:
            dbg[nme] = nc.dram_tensor(nme, shape, dt, kind="ExternalOutput").ap()

    pid = nc.partition_id([ET.SP])
    qv0 = (pid % 4) * QT
    qrow0 = pid % 4

    p = Prog()

    def OP(eng, reads, writes, fn, dma=False, cc=False):
        p.op(eng, fn, reads=reads, writes=writes, dma=dma, cc=cc)
        if SERIAL:
            p.fence()

    def DMA(eng, out_ap, in_ap, reads, writes):
        OP(eng, reads, writes,
           lambda e, o=out_ap, i=in_ap: e.dma_start(out=o, in_=i), dma=True)

    def tracked(es, names):
        def sb(nme, shape, dt=F32):
            names.append(nme)
            return es.enter_context(nc.sbuf_tensor(nme, shape, dt))
        return sb

    with ExitStack() as es0:
        sb0 = lambda nme, shape, dt=F32: es0.enter_context(nc.sbuf_tensor(nme, shape, dt))
        ps0 = lambda nme, shape, dt=F32: es0.enter_context(nc.psum_tensor(nme, shape, dt))

        bcdw_sb = sb0("bcdw_sb", [128, NDC, 96], BF16)
        dupw_sb = sb0("dupw_sb", [DR, DSL], BF16)
        convw_sb = sb0("convw_sb", [128, NDC * KC])
        convb_sb = sb0("convb_sb", [128, NDC])
        dupb_sb = sb0("dupb_sb", [128, NDC])
        lmA_sb = sb0("lmA_sb", [128, NDC * NST])
        A_sb = sb0("A_sb", [128, NDC * NST])
        dparam_sb = sb0("dparam_sb", [128, NDC])
        rmsw_sb = sb0("rmsw_sb", [128, 8])
        w1s_sb = sb0("w1s_sb", [128, 32])
        w2s_sb = sb0("w2s_sb", [128, 32])
        h1s_sb = sb0("h1s_sb", [128, 32])
        h2s_sb = sb0("h2s_sb", [128, 32])
        w3sRep = sb0("w3sRep", [128, D])
        ident = sb0("ident", [128, 128], BF16)
        state = sb0("state", [128, NDC * NST])
        dumpb = sb0("dumpb", [128, D], BF16)
        sqc = sb0("sqc", [128, 4])
        rstd = sb0("rstd", [128, 4])
        s1 = sb0("s1", [128, 8])
        s2 = sb0("s2", [128, 8])
        meanb = sb0("meanb", [128, 8])
        varb = sb0("varb", [128, 8])
        rstd1 = sb0("rstd1", [128, 8])

        p_tr = [ps0(f"p_tr{i}", [128, 1024], BF16) for i in range(2)]
        p_mm = [ps0(f"p_mm{i}", [128, 512]) for i in range(5)]
        p_bcd = ps0("p_bcd", [96, 512])

        sems = {ch: es0.enter_context(
            nc.semaphore("sem_" + "_".join(str(x) for x in ch)))
            for ch in CHANS}
        block = es0.enter_context(nc.Block())

        DMA("gpsimd", bcdw_sb[:], bcdw.rearrange("(c p) m -> p c m", p=128),
            ["bcdw"], [bcdw_sb])
        DMA("gpsimd", dupw_sb[:], dupw, ["dupw"], [dupw_sb])
        DMA("sync", convw_sb[:], convw, ["convw"], [convw_sb])
        DMA("sync", convb_sb[:], convb, ["convb"], [convb_sb])
        DMA("sync", dupb_sb[:], dupb, ["dupb"], [dupb_sb])
        DMA("sync", lmA_sb[:], lmA, ["lmA"], [lmA_sb])
        DMA("sync", dparam_sb[:], dparam, ["dparam"], [dparam_sb])
        DMA("sync", rmsw_sb[:], rmsw, ["rmsw"], [rmsw_sb])
        DMA("sync", w1s_sb[:], w1s, ["w1s"], [w1s_sb])
        DMA("sync", w2s_sb[:], w2s, ["w2s"], [w2s_sb])
        DMA("sync", h1s_sb[:], h1s, ["h1s"], [h1s_sb])
        DMA("sync", h2s_sb[:], h2s, ["h2s"], [h2s_sb])
        DMA("scalar", w3sRep[:], w3s[0:1, :].partition_broadcast(128),
            ["w3s"], [w3sRep])
        OP("gpsimd", [], [state], lambda e: e.memset(state[:], 0.0))
        snapped = {}

        def snap_vals(e):
            snapped["qv"] = e.snap(qv0, min_val=0, max_val=3 * QT)
            return e.nop()
        OP("sync", [], [], snap_vals)
        DMA("gpsimd", ident[:], ident_in, ["ident_in"], [ident])
        OP("scalar", [lmA_sb], [A_sb],
           lambda e: e.activation(out=A_sb[:], in_=lmA_sb[:], func=AF.Exp))
        OP("scalar", [A_sb], [A_sb],
           lambda e: e.mul(A_sb[:], A_sb[:], -1.0))
        constv = [EPS, 1.0, -1.0, -0.5, 2.0, 0.5, 1.0 / D, 0.0]
        const_sb = sb0("const_sb", [128, len(constv)])
        for i, v in enumerate(constv):
            OP("gpsimd", [], [const_sb],
               lambda e, i=i, v=v: e.memset(const_sb[:, i:i + 1], v))
            nc.const_aps.register(F32, v, const_sb[:, i:i + 1])
        p.fence()

        with ExitStack() as esM:
            nmM = []
            sbM = tracked(esM, nmM)
            xs_sub = sbM("xs_sub", [128, NDC, 515])
            xc2 = [sbM(f"xc{j}", [128, NDC, LC]) for j in range(2)]
            delta2 = [[sbM(f"delta{j}_{dc}", [128, LC]) for dc in range(NDC)]
                      for j in range(2)]
            u_dc = [sbM(f"u{dc}", [128, LC]) for dc in range(NDC)]
            y_dc = [sbM(f"yac{dc}", [128, LC]) for dc in range(NDC)]
            bcd2 = [sbM(f"bcd{j}", [96, LC]) for j in range(2)]
            sres2 = [sbM(f"sres{j}", [128, NDC, LC], BF16) for j in range(2)]
            dlr2 = [sbM("dlr0", [DR, LC], BF16)] * 2
            x_tm = sbM("x_tm", [128, 4, D])
            xhat2 = [sbM(f"xhat{i}", [128, D], BF16) for i in range(2)]
            xn_fm = sbM("xn_fm", [128, 8, 512], BF16)
            xcb = sbM("xcb", [128, NDC, 512], BF16)
            conv_t2 = [sbM(f"conv_t{i}", [128, 512]) for i in range(2)]
            conv_t = conv_t2[0]
            wi_t = sbM("wi_t", [128, 8, 1024], BF16)
            gath = sbM("gath", [96, 2, LC])
            sp_t1 = sbM("sp_t1", [128, 512])
            sp_t2 = conv_t
            bmcm2 = [sbM(f"bmcm{i}", [128, 2, LC]) for i in range(2)]
            abuf = [sbM(f"abuf{i}", [128, LC]) for i in range(2)]
            bxbuf = [sbM(f"bxbuf{i}", [128, LC]) for i in range(2)]
            hbuf = [sbM(f"hbuf{i}", [128, LC]) for i in range(2)]
            wwbuf = [sbM(f"wwbuf{i}", [128, LC]) for i in range(2)]
            ybar_t = sbM("ybar_t", [128, LC], BF16)
            wout_t = sbM("wout_t", [128, 16, D], BF16)
            yall = sbM("yall", [128, 16, QT], BF16)
            x_qt = sbM("x_qt", [128, 1, D])
            zb = sbM("zb", [128, 1, D])
            x1bb = sbM("x1bb", [128, 1, D], BF16)
            if debug:
                print("sbuf watermark after mamba decls:", nc.sbuf_base,
                      "top:", nc.sbuf_top)
            OP("gpsimd", [], [xs_sub], lambda e: e.memset(xs_sub[:, :, 0:3], 0.0))
            DMA("gpsimd", wi_t[:], wi.rearrange("(k p) m -> p k m", p=128),
                ["wi"], [wi_t])
            DMA("gpsimd", wout_t[:], wout.rearrange("(k p) m -> p k m", p=128),
                ["wout"], [wout_t])

            def genA(ci):
                T0 = ci * LC
                xc = xc2[ci % 2]
                bcd = bcd2[ci % 2]
                sres = sres2[ci % 2]
                DMA("sync", x_tm[:],
                    xin[T0:T0 + 512, :].rearrange("(j p) d -> p j d", p=128),
                    ["x_seq"], [x_tm])
                yield
                for j in range(4):
                    OP("scalar", [x_tm], [sqc, dumpb],
                       lambda e, j=j: e.activation(
                           out=dumpb[:], in_=x_tm[:, j], func=AF.Square,
                           accum_out=sqc[:, j:j + 1]))
                OP("scalar", [sqc], [rstd],
                   lambda e: e.activation(out=rstd[:], in_=sqc[:],
                                          func=AF.Ln, scale=1.0 / D, bias=EPS))
                OP("scalar", [rstd], [rstd],
                   lambda e: e.activation(out=rstd[:], in_=rstd[:],
                                          func=AF.Exp, scale=-0.5))
                yield
                for j in range(4):
                    xh = xhat2[j % 2]
                    OP("scalar", [x_tm, rstd], [xh],
                       lambda e, j=j, xh=xh: e.activation(
                           out=xh[:], in_=x_tm[:, j], func=AF.Identity,
                           scale=rstd[:, j:j + 1]))
                    for dc8 in range(8):
                        pt = p_tr[dc8 % 2]
                        OP("tensor", [xh, ident], [pt],
                           lambda e, dc8=dc8, pt=pt, xh=xh: e.transpose(
                               pt[:, 0:128],
                               xh[:, dc8 * 128:(dc8 + 1) * 128],
                               ident[:]))
                        OP("scalar", [pt, rmsw_sb], [xn_fm],
                           lambda e, j=j, dc8=dc8, pt=pt: e.activation(
                               out=xn_fm[:, dc8, j * 128:(j + 1) * 128],
                               in_=pt[:, 0:128], func=AF.Identity,
                               scale=rmsw_sb[:, dc8:dc8 + 1]))
                    yield
                def mk_mm(m):
                    pm = p_mm[m % 2]

                    def mmg(e, m=m, pm=pm):
                        r = None
                        for k in range(8):
                            r = e.matmul(
                                pm[:], wi_t[:, k, m * 128:(m + 1) * 128],
                                xn_fm[:, k, :], start=(k == 0), stop=(k == 7))
                        return r
                    OP("tensor", [wi_t, xn_fm], [pm], mmg)

                def mk_evac(m):
                    pm = p_mm[m % 2]
                    if m < 4:
                        OP("scalar", [pm], [xs_sub],
                           lambda e, m=m, pm=pm: e.activation(
                               out=xs_sub[:, m, 3:515], in_=pm[:],
                               func=AF.Identity))
                    else:
                        OP("scalar", [pm], [sres],
                           lambda e, m=m, pm=pm, sres=sres: e.activation(
                               out=sres[:, m - 4, :], in_=pm[:],
                               func=AF.Silu))

                def mk_conv(dc):
                    ct = conv_t2[dc % 2]
                    OP("vector", [xs_sub, convw_sb], [ct],
                       lambda e, dc=dc, ct=ct: e.tensor_scalar(
                           out=ct[:], in0=xs_sub[:, dc, 0:512],
                           scalar1=convw_sb[:, dc * KC:dc * KC + 1],
                           scalar2=None, op0=ALU.mult))
                    for k in range(1, KC):
                        OP("vector", [xs_sub, convw_sb, ct], [ct],
                           lambda e, dc=dc, k=k, ct=ct: e.scalar_tensor_tensor(
                               out=ct[:], in0=xs_sub[:, dc, k:k + 512],
                               scalar=convw_sb[:, dc * KC + k:dc * KC + k + 1],
                               in1=ct[:], op0=ALU.mult, op1=ALU.add))

                def mk_silu(dc):
                    ct = conv_t2[dc % 2]
                    OP("scalar", [ct, convb_sb], [xc],
                       lambda e, dc=dc, xc=xc, ct=ct: e.activation(
                           out=xc[:, dc, :], in_=ct[:],
                           func=AF.Silu, bias=convb_sb[:, dc:dc + 1]))
                    OP("scalar", [xc], [xcb],
                       lambda e, dc=dc, xc=xc: e.activation(
                           out=xcb[:, dc, :], in_=xc[:, dc, :],
                           func=AF.Identity))

                mk_mm(0)
                mk_mm(1)
                yield
                mk_evac(0)
                mk_evac(1)
                mk_mm(2)
                mk_mm(3)
                yield
                mk_evac(2)
                mk_evac(3)
                mk_conv(0)
                mk_conv(1)
                yield
                mk_silu(0)
                mk_silu(1)
                mk_conv(2)
                mk_conv(3)
                yield
                mk_silu(2)
                mk_silu(3)
                if debug and ci == 0:
                    DMA("sync", dbg["d_xc"], xc[:].rearrange("p a l -> p (a l)"),
                        [xc], ["d_xc"])
                    DMA("sync", dbg["d_xn"],
                        xn_fm[:].rearrange("p a l -> p (a l)"),
                        [xn_fm], ["d_xn"])
                OP(BISECT1, [xs_sub], [xs_sub],
                   lambda e: e.tensor_copy(xs_sub[:, :, 0:3],
                                           xs_sub[:, :, 512:515]))

                def bcdg(e):
                    r = None
                    for dc in range(NDC):
                        r = e.matmul(p_bcd[:], bcdw_sb[:, dc, :], xcb[:, dc, :],
                                     start=(dc == 0), stop=(dc == NDC - 1))
                    return r
                OP("tensor", [bcdw_sb, xcb], [p_bcd], bcdg)
                yield
                OP("scalar", [p_bcd], [bcd],
                   lambda e, bcd=bcd: e.activation(
                       out=bcd[:, :], in_=p_bcd[0:96, :], func=AF.Identity))
                DMA("sync", cc_bcd_in, bcd[:], [bcd], ["cc_bcd_in"])
                OP("gpsimd", ["cc_bcd_in", "cc_chain"],
                   ["cc_bcd_out", "cc_chain"],
                   lambda e: e.collective_compute(
                       "AllGather", ALU.bypass,
                       replica_groups=[[0, 1, 2, 3], [4, 5, 6, 7]],
                       ins=[cc_bcd_in], outs=[cc_bcd_out]), cc=True)
                mk_mm(4)
                mk_mm(5)
                yield
                mk_evac(4)
                mk_evac(5)
                mk_mm(6)
                mk_mm(7)
                yield
                mk_evac(6)
                mk_evac(7)
                yield

            def emitGather(ci):
                bcd = bcd2[ci % 2]
                dlr_bf = dlr2[ci % 2]
                bcs = bc_scr2[ci % 2]
                bcs_nm = f"bc_scr{ci % 2}"
                DMA("sync", gath[:],
                    cc_bcd_out.rearrange("(r q) l -> q r l", r=4)[:, 0:2],
                    ["cc_bcd_out"], [gath])
                OP(BISECT1, [gath], [bcd],
                   lambda e, bcd=bcd: e.tensor_tensor(
                       out=bcd[:], in0=gath[:, 0], in1=gath[:, 1], op=ALU.add))
                DMA("sync", gath[:],
                    cc_bcd_out.rearrange("(r q) l -> q r l", r=4)[:, 2:4],
                    ["cc_bcd_out", bcd], [gath])
                OP(BISECT1, [gath, bcd], [bcd],
                   lambda e, bcd=bcd: e.tensor_tensor(
                       out=bcd[:], in0=bcd[:], in1=gath[:, 0], op=ALU.add))
                OP(BISECT1, [gath, bcd], [bcd],
                   lambda e, bcd=bcd: e.tensor_tensor(
                       out=bcd[:], in0=bcd[:], in1=gath[:, 1], op=ALU.add))
                OP(BISECT1, [bcd], [dlr_bf],
                   lambda e, bcd=bcd, dlr_bf=dlr_bf: e.tensor_copy(
                       dlr_bf[:], bcd[0:DR, :]))
                DMA("sync", bcs[:, 0:LC], bcd[DR:DR + 16, :],
                    [bcd], [bcs_nm])
                DMA("sync", bcs[:, LC:2 * LC], bcd[DR + 16:96, :],
                    [bcd], [bcs_nm])
                if debug and ci == 0:
                    DMA("sync", dbg["d_bcd"], bcd[:], [bcd], ["d_bcd"])

            def emitDup(ci, m):
                dlr_bf = dlr2[ci % 2]
                delta_m = delta2[ci % 2][m]
                pm = p_mm[2]
                OP("tensor", [dupw_sb, dlr_bf], [pm],
                   lambda e, m=m, pm=pm, dlr_bf=dlr_bf: e.matmul(
                       pm[:], dupw_sb[:, m * 128:(m + 1) * 128],
                       dlr_bf[:, :], start=True, stop=True))
                OP("scalar", [pm, dupb_sb], [sp_t1],
                   lambda e, m=m, pm=pm: e.activation(
                       out=sp_t1[:], in_=pm[:], func=AF.Abs,
                       bias=dupb_sb[:, m:m + 1]))
                OP("scalar", [sp_t1], [sp_t1],
                   lambda e: e.activation(out=sp_t1[:], in_=sp_t1[:],
                                          func=AF.Exp, scale=-1.0))
                OP("scalar", [sp_t1], [sp_t1],
                   lambda e: e.activation(out=sp_t1[:], in_=sp_t1[:],
                                          func=AF.Ln, bias=1.0))
                OP("scalar", [pm, dupb_sb], [sp_t2],
                   lambda e, m=m, pm=pm: e.activation(
                       out=sp_t2[:], in_=pm[:], func=AF.Relu,
                       bias=dupb_sb[:, m:m + 1]))
                OP(BISECT1, [sp_t1, sp_t2], [delta_m],
                   lambda e, delta_m=delta_m: e.tensor_tensor(
                       out=delta_m[:], in0=sp_t1[:], in1=sp_t2[:],
                       op=ALU.add))
                if debug and ci == 0:
                    DMA("sync", dbg["d_delta"][:, m * LC:(m + 1) * LC],
                        delta_m[:], [delta_m], ["d_delta"])

            def emitUY(ci, m):
                xc = xc2[ci % 2]
                delta_m = delta2[ci % 2][m]
                OP(BISECT1, [delta_m, xc], [u_dc[m]],
                   lambda e, m=m, xc=xc, delta_m=delta_m: e.tensor_tensor(
                       out=u_dc[m][:], in0=delta_m[:],
                       in1=xc[:, m, :], op=ALU.mult))
                OP("vector", [xc, dparam_sb], [y_dc[m]],
                   lambda e, m=m, xc=xc: e.tensor_scalar(
                       out=y_dc[m][:], in0=xc[:, m, :],
                       scalar1=dparam_sb[:, m:m + 1], scalar2=None,
                       op0=ALU.mult))

            def emitBcast(ci, n):
                bcs = bc_scr2[ci % 2]
                bcs_nm = f"bc_scr{ci % 2}"
                bmcm = bmcm2[n % 2]
                OP("sync", [bcs_nm, "bc_chain"], [bmcm, "bc_chain"],
                   lambda e, bcs=bcs, n=n, bmcm=bmcm: e.dma_start(
                       out=bmcm[:],
                       in_=bcs[n:n + 1, :].partition_broadcast(128)),
                   dma=True)

            def emitScanGroup(ci, n):
                bmcm = bmcm2[n % 2]
                bm = bmcm[:, 0, :]
                cm = bmcm[:, 1, :]
                delta_c = delta2[ci % 2]
                for dc in range(NDC):
                    i2 = (n * NDC + dc) % 2
                    ab, bx, hh, ww = abuf[i2], bxbuf[i2], hbuf[i2], wwbuf[i2]
                    col = dc * NST + n
                    OP("scalar", [delta_c[dc], A_sb], [ab],
                       lambda e, dc=dc, col=col, ab=ab, delta_c=delta_c:
                       e.activation(
                           out=ab[:], in_=delta_c[dc][:], func=AF.Exp,
                           scale=A_sb[:, col:col + 1]))
                    OP("vector", [u_dc[dc], bm], [bx],
                       lambda e, dc=dc, bm=bm, bx=bx: e.tensor_tensor(
                           out=bx[:], in0=u_dc[dc][:], in1=bm[:],
                           op=ALU.mult))
                    p.op("vector",
                         lambda e, ab=ab, bx=bx, hh=hh, col=col:
                         e.tensor_tensor_scan(
                             out=hh[:], data0=ab[:], data1=bx[:],
                             initial=state[:, col:col + 1],
                             op0=ALU.mult, op1=ALU.add),
                         reads=[ab, bx, state], writes=[hh],
                         fifo_unsafe=True)
                    OP("vector", [hh], [state],
                       lambda e, hh=hh, col=col: e.tensor_copy(
                           state[:, col:col + 1], hh[:, LC - 1:LC]))
                    ymul_eng = "gpsimd" if dc % 2 == 1 else "vector"
                    OP(ymul_eng, [hh, cm], [ww],
                       lambda e, hh=hh, cm=cm, ww=ww: e.tensor_tensor(
                           out=ww[:], in0=hh[:], in1=cm[:], op=ALU.mult))
                    yadd_eng = "gpsimd" if dc % 2 == 0 else "vector"
                    OP(yadd_eng, [ww, y_dc[dc]], [y_dc[dc]],
                       lambda e, ww=ww, dc=dc: e.tensor_tensor(
                           out=y_dc[dc][:], in0=y_dc[dc][:],
                           in1=ww[:], op=ALU.add))

            def emitChunkTail(ci):
                sres = sres2[ci % 2]
                ccyi, ccyo = cc_y_in2[ci % 2], cc_y_out2[ci % 2]
                ccyi_nm, ccyo_nm = f"cc_y_in{ci % 2}", f"cc_y_out{ci % 2}"
                if debug and ci == 0:
                    for m in range(NDC):
                        DMA("sync", dbg["d_yacc"][:, m * LC:(m + 1) * LC],
                            y_dc[m][:], [y_dc[m]], ["d_yacc"])
                for dc in range(NDC):
                    OP("vector", [y_dc[dc], sres], [ybar_t],
                       lambda e, dc=dc, sres=sres: e.tensor_tensor(
                           out=ybar_t[:], in0=y_dc[dc][:],
                           in1=sres[:, dc, :], op=ALU.mult))
                    DMA("sync", ccyi[dc * 128:(dc + 1) * 128, :], ybar_t[:],
                        [ybar_t], [ccyi_nm])
                    if ci + 1 < EMIT_CHUNKS:
                        emitUY(ci + 1, dc)

            def emitAGy(ci):
                ccyi, ccyo = cc_y_in2[ci % 2], cc_y_out2[ci % 2]
                ccyi_nm, ccyo_nm = f"cc_y_in{ci % 2}", f"cc_y_out{ci % 2}"
                OP("gpsimd", [ccyi_nm, "cc_chain"], [ccyo_nm, "cc_chain"],
                   lambda e, ccyi=ccyi, ccyo=ccyo: e.collective_compute(
                       "AllGather", ALU.bypass,
                       replica_groups=[[0, 1, 2, 3], [4, 5, 6, 7]],
                       ins=[ccyi], outs=[ccyo]), cc=True)

            def genPost(ci):
                ccyo = cc_y_out2[ci % 2]
                ccyo_nm = f"cc_y_out{ci % 2}"
                OP("sync", [ccyo_nm], [yall],
                   lambda e, ccyo=ccyo: e.dma_start(
                       out=yall[:],
                       in_=ccyo[:, bass.ds(snapped["qv"], QT)].rearrange(
                           "(k p) t -> p k t", p=128)), dma=True)
                DMA("sync", x_qt[:, 0, :], x_q[ci], ["x_q"], [x_qt])
                if debug and ci == 0:
                    DMA("sync", dbg["d_yall"],
                        yall[:].rearrange("p a l -> p (a l)"), [yall], ["d_yall"])
                yield
                pz = [p_mm[3], p_mm[4]]

                def opg(e, nh):
                    r = None
                    for k in range(16):
                        r = e.matmul(
                            pz[nh][:], yall[:, k, :],
                            wout_t[:, k, nh * 512:(nh + 1) * 512],
                            start=(k == 0), stop=(k == 15))
                    return r
                OP("tensor", [yall, wout_t], [pz[0]],
                   lambda e: opg(e, 0))
                yield
                OP("tensor", [yall, wout_t], [pz[1]],
                   lambda e: opg(e, 1))
                yield
                for nh in range(2):
                    OP("vector", [x_qt, pz[nh]], [zb],
                       lambda e, nh=nh, pz=pz: e.scalar_tensor_tensor(
                           out=zb[:, 0, nh * 512:(nh + 1) * 512],
                           in0=x_qt[:, 0, nh * 512:(nh + 1) * 512],
                           scalar=2.0, in1=pz[nh][:], op0=ALU.mult,
                           op1=ALU.add))
                OP("scalar", [zb], [s2, dumpb],
                   lambda e: e.activation(
                       out=dumpb[:], in_=zb[:, 0], func=AF.Square,
                       accum_out=s2[:, 0:1]))
                yield
                OP("vector", [zb], [s1],
                   lambda e: e.tensor_reduce(
                       out=s1[:, 0:1], in_=zb[:, 0], axis=AXL.X, op=ALU.add))
                OP("vector", [s1], [meanb],
                   lambda e: e.tensor_scalar(out=meanb[:, 0:1], in0=s1[:, 0:1],
                                             scalar1=1.0 / D, scalar2=None,
                                             op0=ALU.mult))
                OP("vector", [meanb], [varb],
                   lambda e: e.tensor_tensor(out=varb[:, 0:1], in0=meanb[:, 0:1],
                                             in1=meanb[:, 0:1], op=ALU.mult))
                OP("vector", [s2, varb], [varb],
                   lambda e: e.scalar_tensor_tensor(
                       out=varb[:, 0:1], in0=s2[:, 0:1], scalar=1.0 / D,
                       in1=varb[:, 0:1], op0=ALU.mult, op1=ALU.subtract))
                OP("scalar", [varb], [rstd1],
                   lambda e: e.activation(out=rstd1[:, 0:1], in_=varb[:, 0:1],
                                          func=AF.Ln, bias=EPS))
                OP("scalar", [rstd1], [rstd1],
                   lambda e: e.activation(out=rstd1[:, 0:1], in_=rstd1[:, 0:1],
                                          func=AF.Exp, scale=-0.5))
                yield
                OP("vector", [zb, meanb, rstd1], [zb],
                   lambda e: e.tensor_scalar(
                       out=zb[:, 0], in0=zb[:, 0],
                       scalar1=meanb[:, 0:1], scalar2=rstd1[:, 0:1],
                       op0=ALU.subtract, op1=ALU.mult))
                OP("scalar", [zb], [x1bb],
                   lambda e: e.activation(out=x1bb[:, 0], in_=zb[:, 0],
                                          func=AF.Identity))
                if debug and ci == 0:
                    DMA("sync", dbg["d_x1"], zb[:, 0, :], [zb], ["d_x1"])
                DMA("sync", x1_spill[ci * QT:(ci + 1) * QT, :], zb[:, 0, :],
                    [zb], ["x1_spill"])
                DMA("sync", x1b_spill[ci * QT:(ci + 1) * QT, :], x1bb[:, 0, :],
                    [x1bb], ["x1b_spill"])
                yield

            def drain(g, k=1):
                if g is not None:
                    for _ in range(k):
                        if next(g, "END") == "END":
                            return

            drain(genA(0), 99)
            emitGather(0)
            for m in range(NDC):
                emitDup(0, m)
                emitUY(0, m)
            emitBcast(0, 0)
            emitBcast(0, 1)
            slotA = {0: 1, 1: 1, 2: 1, 3: 1, 4: 1, 5: 1, 6: 1, 7: 1,
                     8: 1, 9: 1, 10: 1, 11: 1, 12: 1, 13: 1}
            slotP = {4: 1, 8: 1, 9: 1, 10: 1, 11: 1, 12: 1}
            gP = None
            for ci in range(EMIT_CHUNKS):
                gA = genA(ci + 1) if ci + 1 < EMIT_CHUNKS else None
                for n in range(NST):
                    if n == 2 and ci >= 1:
                        emitAGy(ci - 1)
                    drain(gP, slotP.get(n, 0))
                    drain(gA, slotA.get(n, 0))
                    if ci + 1 < EMIT_CHUNKS:
                        if n == 13:
                            emitGather(ci + 1)
                        elif n == 14:
                            emitDup(ci + 1, 0)
                            emitDup(ci + 1, 1)
                        elif n == 15:
                            emitDup(ci + 1, 2)
                            emitDup(ci + 1, 3)
                    emitScanGroup(ci, n)
                    if n + 2 < NST:
                        emitBcast(ci, n + 2)
                    elif ci + 1 < EMIT_CHUNKS:
                        emitBcast(ci + 1, n + 2 - NST)
                emitChunkTail(ci)
                drain(gP, 99)
                gP = genPost(ci)
            emitAGy(EMIT_CHUNKS - 1)
            drain(gP, 99)

        p.close_fence(nmM)
        if EMIT_FFN:
            with ExitStack() as esF:
                nmF = []
                sbF = tracked(esF, nmF)
                gbuf = sbF("gbuf", [128, 32, FSUB], BF16)
                t1_sb = sbF("t1_sb", [128, 512])
                ztmp = sbF("ztmp", [128, 512])
                out_t = sbF("out_t", [128, D])
                for sub in range(NFS):
                    R0 = sub * FSUB
                    if sub > 0:
                        p.close_fence([n + str(sub - 1) for n in
                                       ("w3rhs", "z2all", "x1row0", "x1row1")])
                    es123 = ExitStack()
                    nm123 = []
                    _sb123 = tracked(es123, nm123)
                    sb123 = lambda nme, shape, dt=BF16: _sb123(
                        nme + str(sub), shape, dt)
                    hobuf = sb123("hobuf", [128, 32, FSUB])
                    x1_fm = sb123("x1_fm", [128, 8, FSUB])
                    wblk = [sb123(f"wblk{i}", [128, 8, 128]) for i in range(4)]
                    with ExitStack() as esX:
                        x1tmb = esX.enter_context(
                            nc.sbuf_tensor(f"x1tmb{sub}", [128, 8, D], BF16))
                        DMA("sync", x1tmb[:],
                            x1b_spill[R0:R0 + FSUB, :].rearrange(
                                "(j p) d -> p j d", p=128), ["x1b_spill"], [x1tmb])
                        for dc8 in range(8):
                            for j in range(8):
                                pt = p_tr[(dc8 * 8 + j) % 2]
                                OP("tensor", [x1tmb, ident], [pt],
                                   lambda e, j=j, dc8=dc8, pt=pt: e.transpose(
                                       pt[:, 0:128], x1tmb[:, j, dc8 * 128:(dc8 + 1) * 128],
                                       ident[:]))
                                OP("scalar", [pt], [x1_fm],
                                   lambda e, j=j, dc8=dc8, pt=pt: e.activation(
                                       out=x1_fm[:, dc8, j * 128:(j + 1) * 128],
                                       in_=pt[:, 0:128], func=AF.Identity))
                    p.close_fence([f"x1tmb{sub}"])
                    for m in range(32):
                        wb1, wb2 = wblk[2 * (m % 2)], wblk[2 * (m % 2) + 1]
                        DMA("gpsimd", wb1[:],
                            w1t[:, m * 128:(m + 1) * 128].rearrange(
                                "(k p) m -> p k m", p=128), ["w1t"], [wb1])
                        DMA("gpsimd", wb2[:],
                            w2t[:, m * 128:(m + 1) * 128].rearrange(
                                "(k p) m -> p k m", p=128), ["w2t"], [wb2])
                        for tt in range(2):
                            pa, pb = p_mm[tt], p_mm[2 + tt]

                            def g1(e, wb1=wb1, tt=tt, pa=pa):
                                r = None
                                for k in range(8):
                                    r = e.matmul(
                                        pa[:], wb1[:, k, :],
                                        x1_fm[:, k, tt * 512:(tt + 1) * 512],
                                        start=(k == 0), stop=(k == 7))
                                return r

                            def g2(e, wb2=wb2, tt=tt, pb=pb):
                                r = None
                                for k in range(8):
                                    r = e.matmul(
                                        pb[:], wb2[:, k, :],
                                        x1_fm[:, k, tt * 512:(tt + 1) * 512],
                                        start=(k == 0), stop=(k == 7))
                                return r
                            OP("tensor", [wb1, x1_fm], [pa], g1)
                            OP("tensor", [wb2, x1_fm], [pb], g2)
                            OP("scalar", [pa, w1s_sb], [t1_sb],
                               lambda e, m=m, pa=pa: e.activation(
                                   out=t1_sb[:], in_=pa[:], func=AF.Silu,
                                   scale=w1s_sb[:, m:m + 1]))
                            OP("vector", [pb, w2s_sb, t1_sb], [gbuf],
                               lambda e, m=m, tt=tt, pb=pb: e.scalar_tensor_tensor(
                                   out=gbuf[:, m, tt * 512:(tt + 1) * 512],
                                   in0=pb[:], scalar=w2s_sb[:, m:m + 1],
                                   in1=t1_sb[:], op0=ALU.mult, op1=ALU.mult))
                    if debug and sub == 0:
                        DMA("sync", dbg["d_x1s"], x1b_spill, ["x1b_spill"], ["d_x1s"])
                        DMA("sync", dbg["d_g"],
                            gbuf[:].rearrange("p a b -> p (a b)"), [gbuf], ["d_g"])
                    for st_i, (wt_ap, wt_name, ssb, src, dst) in enumerate([
                            (h1t, "h1t", h1s_sb, gbuf, hobuf),
                            (h2t, "h2t", h2s_sb, hobuf, gbuf)]):
                        with ExitStack() as esH:
                            hblk = [esH.enter_context(nc.sbuf_tensor(
                                f"hblk{sub}_{st_i}_{i}", [128, 32, 128], BF16))
                                for i in range(2)]
                            for m in range(32):
                                hb = hblk[m % 2]
                                DMA("gpsimd", hb[:],
                                    wt_ap[:, m * 128:(m + 1) * 128].rearrange(
                                        "(k p) m -> p k m", p=128), [wt_name], [hb])
                                for tt in range(2):
                                    pa = p_mm[tt]

                                    def hg(e, hb=hb, tt=tt, pa=pa, src=src):
                                        r = None
                                        for k in range(32):
                                            r = e.matmul(
                                                pa[:], hb[:, k, :],
                                                src[:, k, tt * 512:(tt + 1) * 512],
                                                start=(k == 0), stop=(k == 31))
                                        return r
                                    OP("tensor", [hb, src], [pa], hg)
                                    OP("scalar", [pa, ssb], [dst],
                                       lambda e, m=m, tt=tt, pa=pa, ssb=ssb,
                                       dst=dst: e.activation(
                                           out=dst[:, m, tt * 512:(tt + 1) * 512],
                                           in_=pa[:], func=AF.Gelu,
                                           scale=ssb[:, m:m + 1]))
                        p.close_fence([f"hblk{sub}_{st_i}_{i}"
                                       for i in range(2)])
                    if debug and sub == 0:
                        DMA("sync", dbg["d_h1"],
                            hobuf[:].rearrange("p a b -> p (a b)"), [hobuf], ["d_h1"])
                        DMA("sync", dbg["d_h2"],
                            gbuf[:].rearrange("p a b -> p (a b)"), [gbuf], ["d_h2"])
                    es123.close()
                    p.close_fence(nm123)
                    with ExitStack() as esW:
                        nmW = []
                        _sbW = tracked(esW, nmW)
                        sbW = lambda nme, shape, dt=F32: _sbW(
                            nme + str(sub), shape, dt)
                        w3rhs = sbW("w3rhs", [128, 32, 512], BF16)
                        z2all = sbW("z2all", [128, 8, D])
                        x1row = [sbW(f"x1row{i}", [128, D]) for i in range(2)]
                        for nh in range(2):
                            DMA("gpsimd", w3rhs[:],
                                w3t[:, nh * 512:(nh + 1) * 512].rearrange(
                                    "(k p) m -> p k m", p=128), ["w3t"], [w3rhs])
                            for tc in range(8):
                                pzn = p_mm[tc % 2]

                                def w3g(e, tc=tc, pzn=pzn):
                                    r = None
                                    for k in range(32):
                                        r = e.matmul(
                                            pzn[:],
                                            gbuf[:, k, tc * 128:(tc + 1) * 128],
                                            w3rhs[:, k, :],
                                            start=(k == 0), stop=(k == 31))
                                    return r
                                OP("tensor", [gbuf, w3rhs], [pzn], w3g)
                                xr = x1row[tc % 2]
                                OP("sync", ["x1_spill"], [xr],
                                   lambda e, tc=tc, R0=R0, xr=xr: e.dma_start(
                                       out=xr[:],
                                       in_=x1_spill[R0 + tc * 128:
                                                    R0 + (tc + 1) * 128, :]),
                                   dma=True)
                                OP("vector", [pzn, w3sRep], [ztmp],
                                   lambda e, nh=nh, pzn=pzn: e.tensor_tensor(
                                       out=ztmp[:], in0=pzn[:],
                                       in1=w3sRep[:, nh * 512:(nh + 1) * 512],
                                       op=ALU.mult))
                                OP("vector", [ztmp, xr], [z2all],
                                   lambda e, tc=tc, nh=nh, xr=xr: e.tensor_tensor(
                                       out=z2all[:, tc, nh * 512:(nh + 1) * 512],
                                       in0=ztmp[:],
                                       in1=xr[:, nh * 512:(nh + 1) * 512],
                                       op=ALU.add))
                        if debug and sub == 0:
                            DMA("sync", dbg["d_z2"],
                                z2all[:].rearrange("p a b -> p (a b)"),
                                [z2all], ["d_z2"])
                        for tc in range(8):
                            OP("scalar", [z2all], [s2, dumpb],
                               lambda e, tc=tc: e.activation(
                                   out=dumpb[:], in_=z2all[:, tc], func=AF.Square,
                                   accum_out=s2[:, tc:tc + 1]))
                            OP("vector", [z2all], [s1],
                               lambda e, tc=tc: e.tensor_reduce(
                                   out=s1[:, tc:tc + 1], in_=z2all[:, tc],
                                   axis=AXL.X, op=ALU.add))
                        OP("vector", [s1], [meanb],
                           lambda e: e.tensor_scalar(
                               out=meanb[:], in0=s1[:], scalar1=1.0 / D,
                               scalar2=None, op0=ALU.mult))
                        OP("vector", [meanb], [varb],
                           lambda e: e.tensor_tensor(
                               out=varb[:], in0=meanb[:], in1=meanb[:],
                               op=ALU.mult))
                        OP("vector", [s2, varb], [varb],
                           lambda e: e.scalar_tensor_tensor(
                               out=varb[:], in0=s2[:], scalar=1.0 / D,
                               in1=varb[:], op0=ALU.mult, op1=ALU.subtract))
                        OP("scalar", [varb], [rstd1],
                           lambda e: e.activation(out=rstd1[:], in_=varb[:],
                                                  func=AF.Ln, bias=EPS))
                        OP("scalar", [rstd1], [rstd1],
                           lambda e: e.activation(out=rstd1[:], in_=rstd1[:],
                                                  func=AF.Exp, scale=-0.5))
                        if debug:
                            DMA("sync", dbg["d_st"],
                                s1[:], [s1], ["d_st"])
                            DMA("sync", dbg["d_st2"],
                                s2[:], [s2], ["d_st2"])
                            DMA("sync", dbg["d_rs"],
                                rstd1[:], [rstd1], ["d_rs"])
                        for tc in range(8):
                            OP("vector", [z2all, meanb, rstd1], [out_t],
                               lambda e, tc=tc: e.tensor_scalar(
                                   out=out_t[:], in0=z2all[:, tc],
                                   scalar1=meanb[:, tc:tc + 1],
                                   scalar2=rstd1[:, tc:tc + 1],
                                   op0=ALU.subtract, op1=ALU.mult))
                            DMA("sync", out[R0 + tc * 128:R0 + (tc + 1) * 128, :],
                                out_t[:], [out_t], ["out"])

        p.fence()
        snap = dict(p.cum)
        for e in ENGS:
            p.op(e, lambda en: en.nop(), reads=(), writes=())

        streams = p.finalize()

        def emit(engname):
            def f(e):
                for fn, inc, ch, waits in streams[engname]:
                    for (sch, sv) in waits:
                        e.wait_ge(sems[sch], sv)
                    instr = fn(e)
                    if instr is not None:
                        instr.then_inc(sems[ch], inc)
            return f

        block.sync(emit("sync"))
        block.scalar(emit("scalar"))
        block.vector(emit("vector"))
        block.tensor(emit("tensor"))
        block.gpsimd(emit("gpsimd"))

    return nc


def _stage_inputs(inputs, c):
    g, q = c // 4, c % 4
    d0 = q * DSL
    asc = np.ascontiguousarray
    x = np.asarray(inputs["L0_attn_x"])
    d = {}
    d["x_seq"] = asc(x[g])
    d["x_q"] = asc(x[g].reshape(NCHUNK, 4, QT, D)[:, q])
    wi_full = np.asarray(inputs["L0_attn_in_proj_w"])
    d["wi"] = asc(np.concatenate(
        [wi_full[:, d0:d0 + DSL], wi_full[:, DI + d0:DI + d0 + DSL]], axis=1))
    cw = np.asarray(inputs["L0_attn_conv_w"])[d0:d0 + DSL]
    d["convw"] = asc(cw.reshape(NDC, 128, KC).transpose(1, 0, 2).reshape(128, NDC * KC))
    cb = np.asarray(inputs["L0_attn_conv_b"])[d0:d0 + DSL]
    d["convb"] = asc(cb.reshape(NDC, 128).T)
    d["bcdw"] = asc(np.asarray(inputs["L0_attn_bcd_w"])[d0:d0 + DSL])
    d["dupw"] = asc(np.asarray(inputs["L0_attn_dup_w"])[:, d0:d0 + DSL])
    db = np.asarray(inputs["L0_attn_dup_b"])[d0:d0 + DSL]
    d["dupb"] = asc(db.reshape(NDC, 128).T)
    lm = np.asarray(inputs["L0_attn_log_minus_A"])[d0:d0 + DSL]
    d["lmA"] = asc(lm.reshape(NDC, 128, NST).transpose(1, 0, 2).reshape(128, NDC * NST))
    dp = np.asarray(inputs["L0_attn_D_param"])[d0:d0 + DSL]
    d["dparam"] = asc(dp.reshape(NDC, 128).T)
    rw = np.asarray(inputs["L0_attn_rms_w"])
    d["rmsw"] = asc(rw.reshape(8, 128).T)
    d["wout"] = asc(np.asarray(inputs["L0_attn_out_proj_w"]))
    d["w1t"] = asc(np.asarray(inputs["L0_ffn_w1_w"]).T)
    d["w2t"] = asc(np.asarray(inputs["L0_ffn_w2_w"]).T)
    d["h1t"] = asc(np.asarray(inputs["L0_ffn_h1_w"]).T)
    d["h2t"] = asc(np.asarray(inputs["L0_ffn_h2_w"]).T)
    d["w3t"] = asc(np.asarray(inputs["L0_ffn_w3_w"]).T)
    for nme, key in [("w1s", "L0_ffn_w1_s"), ("w2s", "L0_ffn_w2_s"),
                     ("h1s", "L0_ffn_h1_s"), ("h2s", "L0_ffn_h2_s")]:
        s = np.asarray(inputs[key])
        d[nme] = asc(s.reshape(32, 128).T)
    d["w3s"] = asc(np.asarray(inputs["L0_ffn_w3_s"]).reshape(1, D))
    d["ident_in"] = np.eye(128, dtype=np.float32)
    return d


def kernel(**inputs):
    nc = build(debug=DEBUG)
    in_maps = [_stage_inputs(inputs, c) for c in range(8)]
    res = run_bass_kernel_spmd(nc, in_maps, core_ids=list(range(8)))
    full = np.zeros((B, L, D), np.float32)
    for c in range(8):
        g, q = c // 4, c % 4
        oc = np.asarray(res.results[c]["out"])
        oc = oc.reshape(NCHUNK, QT, D)
        for ci in range(NCHUNK):
            full[g, ci * LC + q * QT:ci * LC + (q + 1) * QT, :] = oc[ci]
    if DEBUG:
        kernel.last_results = res
    return full


# version 20
# speedup vs baseline: 1.1698x; 1.0416x over previous
BISECT1 = "gpsimd"
"""Fused single-launch megakernel for mk_seq_postln_d1 (Mamba block + ternary
FFN, depth 1, x: [2, 8192, 1024]) on 8 trn2 NeuronCores.

Parallelization
  - 2 sequences x 4-core groups. Within a group each core owns a 512-wide
    slice of the DI=2048 inner dim for the whole mamba part (in_proj slice,
    conv, selective scan) over the full 8192-token sequence, so the scan has
    no cross-core sequential chain.
  - Each core owns the q-th 256-token quarter of every 1024-token chunk for
    out_proj / layernorms / FFN (2048 tokens per core).
  - Cross-core data: two AllGather collectives per chunk inside each 4-core
    group: bcd partials [96 x 1024] fp32 and ybar slices [512 x 1024] bf16.
Layouts
  - mamba inner tensors feature-major [d on partitions x tokens free]; scan
    via tensor_tensor_scan along free dim with per-(d-chunk, n) state column.
  - residual / LN / FFN-output pipeline token-major; FFN hidden feature-major.
  - bf16 matmuls with fp32 PSUM accumulation, fp32 elementwise + scan state.
"""

import numpy as np
import concourse.bass as bass
import concourse.mybir as mybir
from concourse.bass_utils import run_bass_kernel_spmd
from contextlib import ExitStack

F32 = mybir.dt.float32
BF16 = mybir.dt.bfloat16
I32 = mybir.dt.int32
AF = mybir.ActivationFunctionType
ALU = mybir.AluOpType
AXL = mybir.AxisListType
ET = mybir.EngineType

B, L, D = 2, 8192, 1024
DI, NST, DR, KC = 2048, 16, 64, 4
DH = 4096
EPS = 1e-5
LC = 512
NCHUNK = L // LC
NSUB = LC // 512
DSL = 512
NDC = DSL // 128
QT = LC // 4
NTOK = NCHUNK * QT
FSUB = 1024
NFS = NTOK // FSUB

ENGS = ("sync", "scalar", "vector", "tensor", "gpsimd")

EMIT_CHUNKS = NCHUNK
DBG_CI = 1
EMIT_FFN = True
DEBUG = False
SERIAL = False


PSLOT = {"sync": 6, "scalar": 1, "gpsimd": 6}
DMAQ = ("sync", "scalar", "gpsimd")
CHANS = tuple((e, "c") for e in ENGS) + tuple(
    (e, "d", i) for e in DMAQ for i in range(max(6, *PSLOT.values()))) + (
    ("gpsimd", "cc"),)


class Prog:

    def __init__(self):
        self.ops = []
        self.cum = {ch: 0 for ch in CHANS}
        self.last_w = {}
        self.readers = {}
        self.fence_lv = {ch: 0 for ch in CHANS}
        self.dma_i = {e: 0 for e in DMAQ}
        self.unsafe = set()

    @staticmethod
    def _name(b):
        if isinstance(b, str):
            return b
        n = getattr(b, "name", None)
        if n is not None:
            return n
        return b.tensor.name

    def op(self, eng, fn, reads=(), writes=(), dma=False, fifo_unsafe=False,
           cc=False):
        if dma:
            i = self.dma_i[eng]
            self.dma_i[eng] = i + 1
            ch = (eng, "d", i % PSLOT[eng])
            inc = 16
        elif cc:
            ch = (eng, "cc")
            inc = 1
        else:
            ch = (eng, "c")
            inc = 1
        waits = {}

        def add(sch, sv):
            if sv <= 0:
                return
            if sch == (eng, "c") and (sch, sv) not in self.unsafe:
                return
            waits[sch] = max(waits.get(sch, 0), sv)

        for b in reads:
            lw = self.last_w.get(self._name(b))
            if lw:
                add(*lw)
        for b in writes:
            nm = self._name(b)
            lw = self.last_w.get(nm)
            if lw:
                add(*lw)
            for r in self.readers.get(nm, ()):
                add(*r)
        for sch, lv in self.fence_lv.items():
            add(sch, lv)
        if dma:
            waits[ch] = max(waits.get(ch, 0), self.cum[ch])
        self.cum[ch] += inc
        me = (ch, self.cum[ch])
        if not dma and eng in ("vector", "gpsimd"):
            self.unsafe.add(me)
        for b in reads:
            self.readers.setdefault(self._name(b), []).append(me)
        for b in writes:
            nm = self._name(b)
            self.last_w[nm] = me
            self.readers[nm] = []
        self.ops.append((eng, fn, inc, ch, waits))

    def fence(self):
        self.fence_lv = dict(self.cum)

    def close_fence(self, names):
        for nm in names:
            for t in ([self.last_w[nm]] if nm in self.last_w else []) +                      list(self.readers.get(nm, ())):
                ch, cum = t
                if self.fence_lv.get(ch, 0) < cum:
                    self.fence_lv[ch] = cum

    def finalize(self):
        covered = {e: {ch: 0 for ch in CHANS} for e in ENGS}
        streams = {e: [] for e in ENGS}
        for eng, fn, inc, ch, waits in self.ops:
            w2 = []
            for sch, sv in sorted(waits.items(), key=lambda kv: str(kv[0])):
                if sv > covered[eng][sch]:
                    w2.append((sch, sv))
                    covered[eng][sch] = sv
            streams[eng].append((fn, inc, ch, w2))
        return streams


def build(debug=False):
    nc = bass.Bass("TRN2", target_bir_lowering=False)

    xin = nc.dram_tensor("x_seq", [L, D], F32, kind="ExternalInput").ap()
    x_q = nc.dram_tensor("x_q", [NCHUNK, QT, D], F32, kind="ExternalInput").ap()
    wi = nc.dram_tensor("wi", [D, 1024], F32, kind="ExternalInput").ap()
    convw = nc.dram_tensor("convw", [128, NDC * KC], F32, kind="ExternalInput").ap()
    convb = nc.dram_tensor("convb", [128, NDC], F32, kind="ExternalInput").ap()
    bcdw = nc.dram_tensor("bcdw", [DSL, 96], F32, kind="ExternalInput").ap()
    dupw = nc.dram_tensor("dupw", [DR, DSL], F32, kind="ExternalInput").ap()
    dupb = nc.dram_tensor("dupb", [128, NDC], F32, kind="ExternalInput").ap()
    lmA = nc.dram_tensor("lmA", [128, NDC * NST], F32, kind="ExternalInput").ap()
    dparam = nc.dram_tensor("dparam", [128, NDC], F32, kind="ExternalInput").ap()
    rmsw = nc.dram_tensor("rmsw", [128, 8], F32, kind="ExternalInput").ap()
    wout = nc.dram_tensor("wout", [DI, D], F32, kind="ExternalInput").ap()
    w1t = nc.dram_tensor("w1t", [D, DH], F32, kind="ExternalInput").ap()
    w2t = nc.dram_tensor("w2t", [D, DH], F32, kind="ExternalInput").ap()
    h1t = nc.dram_tensor("h1t", [DH, DH], F32, kind="ExternalInput").ap()
    h2t = nc.dram_tensor("h2t", [DH, DH], F32, kind="ExternalInput").ap()
    w3t = nc.dram_tensor("w3t", [DH, D], F32, kind="ExternalInput").ap()
    w1s = nc.dram_tensor("w1s", [128, 32], F32, kind="ExternalInput").ap()
    w2s = nc.dram_tensor("w2s", [128, 32], F32, kind="ExternalInput").ap()
    h1s = nc.dram_tensor("h1s", [128, 32], F32, kind="ExternalInput").ap()
    h2s = nc.dram_tensor("h2s", [128, 32], F32, kind="ExternalInput").ap()
    w3s = nc.dram_tensor("w3s", [1, D], F32, kind="ExternalInput").ap()
    ident_in = nc.dram_tensor("ident_in", [128, 128], F32, kind="ExternalInput").ap()
    out = nc.dram_tensor("out", [NTOK, D], F32, kind="ExternalOutput").ap()

    cc_bcd_in = nc.dram_tensor("cc_bcd_in", [96, LC], F32, kind="Internal").ap()
    cc_bcd_out = nc.dram_tensor("cc_bcd_out", [96, LC], F32, kind="Internal").ap()
    cc_y_in2 = [nc.dram_tensor(f"cc_y_in{j}", [DSL, LC], BF16,
                               kind="Internal").ap() for j in range(2)]
    cc_y_out2 = [nc.dram_tensor(f"cc_y_out{j}", [DI, LC], BF16,
                                kind="Internal").ap() for j in range(2)]
    bc_scr2 = [nc.dram_tensor(f"bc_scr{j}", [16, 2 * LC], F32,
                               kind="Internal").ap() for j in range(2)]
    x1_spill = nc.dram_tensor("x1_spill", [NTOK, D], F32, kind="Internal").ap()
    x1b_spill = nc.dram_tensor("x1b_spill", [NTOK, D], BF16, kind="Internal").ap()

    dbg = {}
    if debug:
        for nme, shape, dt in [
            ("d_xn", [128, 8 * 512], BF16),
            ("d_xc", [128, NDC * LC], F32),
            ("d_bcd", [96, LC], F32),
            ("d_delta", [128, NDC * LC], F32),
            ("d_yacc", [128, NDC * LC], F32),
            ("d_x1", [QT, D], F32),
            ("d_yall", [128, 16 * QT], BF16),
            ("d_x1s", [NTOK, D], BF16),
            ("d_g", [128, 32 * FSUB], BF16),
            ("d_h1", [128, 32 * FSUB], BF16),
            ("d_h2", [128, 32 * FSUB], BF16),
            ("d_z2", [128, 8 * D], F32),
            ("d_st", [128, 8], F32),
            ("d_st2", [128, 8], F32),
            ("d_rs", [128, 8], F32),
            ("d_state", [128, NDC * NST], F32),
            ("d_zb", [128, 2 * D], F32),
        ]:
            dbg[nme] = nc.dram_tensor(nme, shape, dt, kind="ExternalOutput").ap()

    pid = nc.partition_id([ET.SP])
    qv0 = (pid % 4) * QT
    qrow0 = pid % 4

    p = Prog()

    def OP(eng, reads, writes, fn, dma=False, cc=False):
        p.op(eng, fn, reads=reads, writes=writes, dma=dma, cc=cc)
        if SERIAL:
            p.fence()

    def DMA(eng, out_ap, in_ap, reads, writes):
        OP(eng, reads, writes,
           lambda e, o=out_ap, i=in_ap: e.dma_start(out=o, in_=i), dma=True)

    def tracked(es, names):
        def sb(nme, shape, dt=F32):
            names.append(nme)
            return es.enter_context(nc.sbuf_tensor(nme, shape, dt))
        return sb

    with ExitStack() as es0:
        sb0 = lambda nme, shape, dt=F32: es0.enter_context(nc.sbuf_tensor(nme, shape, dt))
        ps0 = lambda nme, shape, dt=F32: es0.enter_context(nc.psum_tensor(nme, shape, dt))

        bcdw_sb = sb0("bcdw_sb", [128, NDC, 96], BF16)
        dupw_sb = sb0("dupw_sb", [DR, DSL], BF16)
        convw_sb = sb0("convw_sb", [128, NDC * KC])
        convb_sb = sb0("convb_sb", [128, NDC])
        dupb_sb = sb0("dupb_sb", [128, NDC])
        lmA_sb = sb0("lmA_sb", [128, NDC * NST])
        A_sb = sb0("A_sb", [128, NDC * NST])
        dparam_sb = sb0("dparam_sb", [128, NDC])
        rmsw_sb = sb0("rmsw_sb", [128, 8])
        w1s_sb = sb0("w1s_sb", [128, 32])
        w2s_sb = sb0("w2s_sb", [128, 32])
        h1s_sb = sb0("h1s_sb", [128, 32])
        h2s_sb = sb0("h2s_sb", [128, 32])
        w3sRep = sb0("w3sRep", [128, D])
        ident = sb0("ident", [128, 128], BF16)
        state = sb0("state", [128, NDC * NST])
        dumpb = sb0("dumpb", [128, D], BF16)
        sqc = sb0("sqc", [128, 4])
        rstd = sb0("rstd", [128, 4])
        s1 = sb0("s1", [128, 8])
        s2 = sb0("s2", [128, 8])
        meanb = sb0("meanb", [128, 8])
        varb = sb0("varb", [128, 8])
        rstd1 = sb0("rstd1", [128, 8])

        p_tr = [ps0(f"p_tr{i}", [128, 1024], BF16) for i in range(2)]
        p_mm = [ps0(f"p_mm{i}", [128, 512]) for i in range(5)]
        p_bcd = ps0("p_bcd", [96, 512])

        sems = {ch: es0.enter_context(
            nc.semaphore("sem_" + "_".join(str(x) for x in ch)))
            for ch in CHANS}
        block = es0.enter_context(nc.Block())

        DMA("gpsimd", bcdw_sb[:], bcdw.rearrange("(c p) m -> p c m", p=128),
            ["bcdw"], [bcdw_sb])
        DMA("gpsimd", dupw_sb[:], dupw, ["dupw"], [dupw_sb])
        DMA("sync", convw_sb[:], convw, ["convw"], [convw_sb])
        DMA("sync", convb_sb[:], convb, ["convb"], [convb_sb])
        DMA("sync", dupb_sb[:], dupb, ["dupb"], [dupb_sb])
        DMA("sync", lmA_sb[:], lmA, ["lmA"], [lmA_sb])
        DMA("sync", dparam_sb[:], dparam, ["dparam"], [dparam_sb])
        DMA("sync", rmsw_sb[:], rmsw, ["rmsw"], [rmsw_sb])
        DMA("sync", w1s_sb[:], w1s, ["w1s"], [w1s_sb])
        DMA("sync", w2s_sb[:], w2s, ["w2s"], [w2s_sb])
        DMA("sync", h1s_sb[:], h1s, ["h1s"], [h1s_sb])
        DMA("sync", h2s_sb[:], h2s, ["h2s"], [h2s_sb])
        DMA("scalar", w3sRep[:], w3s[0:1, :].partition_broadcast(128),
            ["w3s"], [w3sRep])
        OP("gpsimd", [], [state], lambda e: e.memset(state[:], 0.0))
        snapped = {}

        def snap_vals(e):
            snapped["qv"] = e.snap(qv0, min_val=0, max_val=3 * QT)
            return e.nop()
        OP("sync", [], [], snap_vals)
        DMA("gpsimd", ident[:], ident_in, ["ident_in"], [ident])
        OP("scalar", [lmA_sb], [A_sb],
           lambda e: e.activation(out=A_sb[:], in_=lmA_sb[:], func=AF.Exp))
        OP("scalar", [A_sb], [A_sb],
           lambda e: e.mul(A_sb[:], A_sb[:], -1.0))
        constv = [EPS, 1.0, -1.0, -0.5, 2.0, 0.5, 1.0 / D, 0.0]
        const_sb = sb0("const_sb", [128, len(constv)])
        for i, v in enumerate(constv):
            OP("gpsimd", [], [const_sb],
               lambda e, i=i, v=v: e.memset(const_sb[:, i:i + 1], v))
            nc.const_aps.register(F32, v, const_sb[:, i:i + 1])
        p.fence()

        with ExitStack() as esM:
            nmM = []
            sbM = tracked(esM, nmM)
            xs_sub = sbM("xs_sub", [128, NDC, 515])
            xc2 = [sbM(f"xc{j}", [128, NDC, LC]) for j in range(2)]
            delta2 = [[sbM(f"delta{j}_{dc}", [128, LC]) for dc in range(NDC)]
                      for j in range(2)]
            u_dc = [sbM(f"u{dc}", [128, LC]) for dc in range(NDC)]
            y_dc = [sbM(f"yac{dc}", [128, LC]) for dc in range(NDC)]
            bcd2 = [sbM(f"bcd{j}", [96, LC]) for j in range(2)]
            sres2 = [sbM(f"sres{j}", [128, NDC, LC], BF16) for j in range(2)]
            dlr2 = [sbM("dlr0", [DR, LC], BF16)] * 2
            x_tm = sbM("x_tm", [128, 4, D])
            xhat2 = [sbM(f"xhat{i}", [128, D], BF16) for i in range(2)]
            xn_fm = sbM("xn_fm", [128, 8, 512], BF16)
            xcb = sbM("xcb", [128, NDC, 512], BF16)
            conv_t2 = [sbM(f"conv_t{i}", [128, 512]) for i in range(2)]
            conv_t = conv_t2[0]
            wi_t = sbM("wi_t", [128, 8, 1024], BF16)
            gath = sbM("gath", [96, 2, LC])
            sp_t1 = sbM("sp_t1", [128, 512])
            sp_t2 = conv_t
            bmcm2 = [sbM(f"bmcm{i}", [128, 2, LC]) for i in range(2)]
            abuf = [sbM(f"abuf{i}", [128, LC]) for i in range(2)]
            bxbuf = [sbM(f"bxbuf{i}", [128, LC]) for i in range(2)]
            hbuf = [sbM(f"hbuf{i}", [128, LC]) for i in range(2)]
            wwbuf = [sbM(f"wwbuf{i}", [128, LC]) for i in range(2)]
            ybar_t = sbM("ybar_t", [128, LC], BF16)
            wout_t = sbM("wout_t", [128, 16, D], BF16)
            yall = sbM("yall", [128, 16, QT], BF16)
            x_qt = sbM("x_qt", [128, 1, D])
            zb = sbM("zb", [128, 1, D])
            x1bb = sbM("x1bb", [128, 1, D], BF16)
            if debug:
                print("sbuf watermark after mamba decls:", nc.sbuf_base,
                      "top:", nc.sbuf_top)
            OP("gpsimd", [], [xs_sub], lambda e: e.memset(xs_sub[:, :, 0:3], 0.0))
            DMA("gpsimd", wi_t[:], wi.rearrange("(k p) m -> p k m", p=128),
                ["wi"], [wi_t])
            DMA("gpsimd", wout_t[:], wout.rearrange("(k p) m -> p k m", p=128),
                ["wout"], [wout_t])

            def genA(ci):
                T0 = ci * LC
                xc = xc2[ci % 2]
                bcd = bcd2[ci % 2]
                sres = sres2[ci % 2]
                DMA("sync", x_tm[:],
                    xin[T0:T0 + 512, :].rearrange("(j p) d -> p j d", p=128),
                    ["x_seq"], [x_tm])
                yield
                for j in range(4):
                    OP("scalar", [x_tm], [sqc, dumpb],
                       lambda e, j=j: e.activation(
                           out=dumpb[:], in_=x_tm[:, j], func=AF.Square,
                           accum_out=sqc[:, j:j + 1]))
                OP("scalar", [sqc], [rstd],
                   lambda e: e.activation(out=rstd[:], in_=sqc[:],
                                          func=AF.Ln, scale=1.0 / D, bias=EPS))
                OP("scalar", [rstd], [rstd],
                   lambda e: e.activation(out=rstd[:], in_=rstd[:],
                                          func=AF.Exp, scale=-0.5))
                yield
                for j in range(4):
                    xh = xhat2[j % 2]
                    OP("scalar", [x_tm, rstd], [xh],
                       lambda e, j=j, xh=xh: e.activation(
                           out=xh[:], in_=x_tm[:, j], func=AF.Identity,
                           scale=rstd[:, j:j + 1]))
                    for dc8 in range(8):
                        pt = p_tr[dc8 % 2]
                        OP("tensor", [xh, ident], [pt],
                           lambda e, dc8=dc8, pt=pt, xh=xh: e.transpose(
                               pt[:, 0:128],
                               xh[:, dc8 * 128:(dc8 + 1) * 128],
                               ident[:]))
                        OP("scalar", [pt, rmsw_sb], [xn_fm],
                           lambda e, j=j, dc8=dc8, pt=pt: e.activation(
                               out=xn_fm[:, dc8, j * 128:(j + 1) * 128],
                               in_=pt[:, 0:128], func=AF.Identity,
                               scale=rmsw_sb[:, dc8:dc8 + 1]))
                    yield
                def mk_mm(m):
                    pm = p_mm[m % 2]

                    def mmg(e, m=m, pm=pm):
                        r = None
                        for k in range(8):
                            r = e.matmul(
                                pm[:], wi_t[:, k, m * 128:(m + 1) * 128],
                                xn_fm[:, k, :], start=(k == 0), stop=(k == 7))
                        return r
                    OP("tensor", [wi_t, xn_fm], [pm], mmg)

                def mk_evac(m):
                    pm = p_mm[m % 2]
                    if m < 4:
                        OP("scalar", [pm], [xs_sub],
                           lambda e, m=m, pm=pm: e.activation(
                               out=xs_sub[:, m, 3:515], in_=pm[:],
                               func=AF.Identity))
                    else:
                        OP("scalar", [pm], [sres],
                           lambda e, m=m, pm=pm, sres=sres: e.activation(
                               out=sres[:, m - 4, :], in_=pm[:],
                               func=AF.Silu))

                def mk_conv(dc):
                    ct = conv_t2[dc % 2]
                    OP("vector", [xs_sub, convw_sb], [ct],
                       lambda e, dc=dc, ct=ct: e.tensor_scalar(
                           out=ct[:], in0=xs_sub[:, dc, 0:512],
                           scalar1=convw_sb[:, dc * KC:dc * KC + 1],
                           scalar2=None, op0=ALU.mult))
                    for k in range(1, KC):
                        OP("vector", [xs_sub, convw_sb, ct], [ct],
                           lambda e, dc=dc, k=k, ct=ct: e.scalar_tensor_tensor(
                               out=ct[:], in0=xs_sub[:, dc, k:k + 512],
                               scalar=convw_sb[:, dc * KC + k:dc * KC + k + 1],
                               in1=ct[:], op0=ALU.mult, op1=ALU.add))

                def mk_silu(dc):
                    ct = conv_t2[dc % 2]
                    OP("scalar", [ct, convb_sb], [xc],
                       lambda e, dc=dc, xc=xc, ct=ct: e.activation(
                           out=xc[:, dc, :], in_=ct[:],
                           func=AF.Silu, bias=convb_sb[:, dc:dc + 1]))
                    OP("scalar", [xc], [xcb],
                       lambda e, dc=dc, xc=xc: e.activation(
                           out=xcb[:, dc, :], in_=xc[:, dc, :],
                           func=AF.Identity))

                mk_mm(0)
                mk_mm(1)
                yield
                mk_evac(0)
                mk_evac(1)
                mk_mm(2)
                mk_mm(3)
                yield
                mk_evac(2)
                mk_evac(3)
                mk_conv(0)
                mk_conv(1)
                yield
                mk_silu(0)
                mk_silu(1)
                mk_conv(2)
                mk_conv(3)
                yield
                mk_silu(2)
                mk_silu(3)
                if debug and ci == 0:
                    DMA("sync", dbg["d_xc"], xc[:].rearrange("p a l -> p (a l)"),
                        [xc], ["d_xc"])
                    DMA("sync", dbg["d_xn"],
                        xn_fm[:].rearrange("p a l -> p (a l)"),
                        [xn_fm], ["d_xn"])
                OP(BISECT1, [xs_sub], [xs_sub],
                   lambda e: e.tensor_copy(xs_sub[:, :, 0:3],
                                           xs_sub[:, :, 512:515]))

                def bcdg(e):
                    r = None
                    for dc in range(NDC):
                        r = e.matmul(p_bcd[:], bcdw_sb[:, dc, :], xcb[:, dc, :],
                                     start=(dc == 0), stop=(dc == NDC - 1))
                    return r
                OP("tensor", [bcdw_sb, xcb], [p_bcd], bcdg)
                OP("vector", [p_bcd], [bcd],
                   lambda e, bcd=bcd: e.tensor_copy(bcd[:, :], p_bcd[0:96, :]))
                DMA("sync", cc_bcd_in, bcd[:], [bcd], ["cc_bcd_in"])
                OP("gpsimd", ["cc_bcd_in", "cc_chain"],
                   ["cc_bcd_out", "cc_chain"],
                   lambda e: e.collective_compute(
                       "AllReduce", ALU.add,
                       replica_groups=[[0, 1, 2, 3], [4, 5, 6, 7]],
                       ins=[cc_bcd_in], outs=[cc_bcd_out]), cc=True)
                yield
                mk_mm(4)
                mk_mm(5)
                yield
                mk_evac(4)
                mk_evac(5)
                mk_mm(6)
                mk_mm(7)
                yield
                mk_evac(6)
                mk_evac(7)
                yield

            def emitGather(ci):
                bcd = bcd2[ci % 2]
                dlr_bf = dlr2[ci % 2]
                bcs = bc_scr2[ci % 2]
                bcs_nm = f"bc_scr{ci % 2}"
                DMA("sync", bcd[:], cc_bcd_out, ["cc_bcd_out"], [bcd])
                OP("gpsimd", [bcd], [dlr_bf],
                   lambda e, bcd=bcd, dlr_bf=dlr_bf: e.tensor_copy(
                       dlr_bf[:], bcd[0:DR, :]))
                DMA("sync", bcs[:, 0:LC], bcd[DR:DR + 16, :],
                    [bcd], [bcs_nm])
                DMA("sync", bcs[:, LC:2 * LC], bcd[DR + 16:96, :],
                    [bcd], [bcs_nm])
                if debug and ci == 0:
                    DMA("sync", dbg["d_bcd"], bcd[:], [bcd], ["d_bcd"])

            def emitDup(ci, m):
                dlr_bf = dlr2[ci % 2]
                delta_m = delta2[ci % 2][m]
                pm = p_mm[2]
                OP("tensor", [dupw_sb, dlr_bf], [pm],
                   lambda e, m=m, pm=pm, dlr_bf=dlr_bf: e.matmul(
                       pm[:], dupw_sb[:, m * 128:(m + 1) * 128],
                       dlr_bf[:, :], start=True, stop=True))
                OP("scalar", [pm, dupb_sb], [sp_t1],
                   lambda e, m=m, pm=pm: e.activation(
                       out=sp_t1[:], in_=pm[:], func=AF.Abs,
                       bias=dupb_sb[:, m:m + 1]))
                OP("scalar", [sp_t1], [sp_t1],
                   lambda e: e.activation(out=sp_t1[:], in_=sp_t1[:],
                                          func=AF.Exp, scale=-1.0))
                OP("scalar", [sp_t1], [sp_t1],
                   lambda e: e.activation(out=sp_t1[:], in_=sp_t1[:],
                                          func=AF.Ln, bias=1.0))
                OP("scalar", [pm, dupb_sb], [sp_t2],
                   lambda e, m=m, pm=pm: e.activation(
                       out=sp_t2[:], in_=pm[:], func=AF.Relu,
                       bias=dupb_sb[:, m:m + 1]))
                OP(BISECT1, [sp_t1, sp_t2], [delta_m],
                   lambda e, delta_m=delta_m: e.tensor_tensor(
                       out=delta_m[:], in0=sp_t1[:], in1=sp_t2[:],
                       op=ALU.add))
                if debug and ci == 0:
                    DMA("sync", dbg["d_delta"][:, m * LC:(m + 1) * LC],
                        delta_m[:], [delta_m], ["d_delta"])

            def emitUY(ci, m):
                xc = xc2[ci % 2]
                delta_m = delta2[ci % 2][m]
                OP(BISECT1, [delta_m, xc], [u_dc[m]],
                   lambda e, m=m, xc=xc, delta_m=delta_m: e.tensor_tensor(
                       out=u_dc[m][:], in0=delta_m[:],
                       in1=xc[:, m, :], op=ALU.mult))
                OP("vector", [xc, dparam_sb], [y_dc[m]],
                   lambda e, m=m, xc=xc: e.tensor_scalar(
                       out=y_dc[m][:], in0=xc[:, m, :],
                       scalar1=dparam_sb[:, m:m + 1], scalar2=None,
                       op0=ALU.mult))

            def emitBcast(ci, n):
                bcs = bc_scr2[ci % 2]
                bcs_nm = f"bc_scr{ci % 2}"
                bmcm = bmcm2[n % 2]
                OP("sync", [bcs_nm, "bc_chain"], [bmcm, "bc_chain"],
                   lambda e, bcs=bcs, n=n, bmcm=bmcm: e.dma_start(
                       out=bmcm[:],
                       in_=bcs[n:n + 1, :].partition_broadcast(128)),
                   dma=True)

            def emitScanGroup(ci, n):
                bmcm = bmcm2[n % 2]
                bm = bmcm[:, 0, :]
                cm = bmcm[:, 1, :]
                delta_c = delta2[ci % 2]
                for dc in range(NDC):
                    i2 = (n * NDC + dc) % 2
                    ab, bx, hh, ww = abuf[i2], bxbuf[i2], hbuf[i2], wwbuf[i2]
                    col = dc * NST + n
                    OP("scalar", [delta_c[dc], A_sb], [ab],
                       lambda e, dc=dc, col=col, ab=ab, delta_c=delta_c:
                       e.activation(
                           out=ab[:], in_=delta_c[dc][:], func=AF.Exp,
                           scale=A_sb[:, col:col + 1]))
                    OP("vector", [u_dc[dc], bm], [bx],
                       lambda e, dc=dc, bm=bm, bx=bx: e.tensor_tensor(
                           out=bx[:], in0=u_dc[dc][:], in1=bm[:],
                           op=ALU.mult))
                    p.op("vector",
                         lambda e, ab=ab, bx=bx, hh=hh, col=col:
                         e.tensor_tensor_scan(
                             out=hh[:], data0=ab[:], data1=bx[:],
                             initial=state[:, col:col + 1],
                             op0=ALU.mult, op1=ALU.add),
                         reads=[ab, bx, state], writes=[hh],
                         fifo_unsafe=True)
                    OP("vector", [hh], [state],
                       lambda e, hh=hh, col=col: e.tensor_copy(
                           state[:, col:col + 1], hh[:, LC - 1:LC]))
                    ymul_eng = "gpsimd" if dc % 2 == 1 else "vector"
                    OP(ymul_eng, [hh, cm], [ww],
                       lambda e, hh=hh, cm=cm, ww=ww: e.tensor_tensor(
                           out=ww[:], in0=hh[:], in1=cm[:], op=ALU.mult))
                    yadd_eng = "gpsimd" if dc % 2 == 0 else "vector"
                    OP(yadd_eng, [ww, y_dc[dc]], [y_dc[dc]],
                       lambda e, ww=ww, dc=dc: e.tensor_tensor(
                           out=y_dc[dc][:], in0=y_dc[dc][:],
                           in1=ww[:], op=ALU.add))

            def emitChunkTail(ci):
                sres = sres2[ci % 2]
                ccyi, ccyo = cc_y_in2[ci % 2], cc_y_out2[ci % 2]
                ccyi_nm, ccyo_nm = f"cc_y_in{ci % 2}", f"cc_y_out{ci % 2}"
                if debug and ci == 0:
                    for m in range(NDC):
                        DMA("sync", dbg["d_yacc"][:, m * LC:(m + 1) * LC],
                            y_dc[m][:], [y_dc[m]], ["d_yacc"])
                for dc in range(NDC):
                    OP("vector", [y_dc[dc], sres], [ybar_t],
                       lambda e, dc=dc, sres=sres: e.tensor_tensor(
                           out=ybar_t[:], in0=y_dc[dc][:],
                           in1=sres[:, dc, :], op=ALU.mult))
                    DMA("sync", ccyi[dc * 128:(dc + 1) * 128, :], ybar_t[:],
                        [ybar_t], [ccyi_nm])
                    if ci + 1 < EMIT_CHUNKS:
                        emitUY(ci + 1, dc)

            def emitAGy(ci):
                ccyi, ccyo = cc_y_in2[ci % 2], cc_y_out2[ci % 2]
                ccyi_nm, ccyo_nm = f"cc_y_in{ci % 2}", f"cc_y_out{ci % 2}"
                OP("gpsimd", [ccyi_nm, "cc_chain"], [ccyo_nm, "cc_chain"],
                   lambda e, ccyi=ccyi, ccyo=ccyo: e.collective_compute(
                       "AllGather", ALU.bypass,
                       replica_groups=[[0, 1, 2, 3], [4, 5, 6, 7]],
                       ins=[ccyi], outs=[ccyo]), cc=True)

            def genPost(ci):
                ccyo = cc_y_out2[ci % 2]
                ccyo_nm = f"cc_y_out{ci % 2}"
                OP("sync", [ccyo_nm], [yall],
                   lambda e, ccyo=ccyo: e.dma_start(
                       out=yall[:],
                       in_=ccyo[:, bass.ds(snapped["qv"], QT)].rearrange(
                           "(k p) t -> p k t", p=128)), dma=True)
                DMA("sync", x_qt[:, 0, :], x_q[ci], ["x_q"], [x_qt])
                if debug and ci == 0:
                    DMA("sync", dbg["d_yall"],
                        yall[:].rearrange("p a l -> p (a l)"), [yall], ["d_yall"])
                yield
                pz = [p_mm[3], p_mm[4]]

                def opg(e, nh):
                    r = None
                    for k in range(16):
                        r = e.matmul(
                            pz[nh][:], yall[:, k, :],
                            wout_t[:, k, nh * 512:(nh + 1) * 512],
                            start=(k == 0), stop=(k == 15))
                    return r
                OP("tensor", [yall, wout_t], [pz[0]],
                   lambda e: opg(e, 0))
                yield
                OP("tensor", [yall, wout_t], [pz[1]],
                   lambda e: opg(e, 1))
                yield
                for nh in range(2):
                    OP("vector", [x_qt, pz[nh]], [zb],
                       lambda e, nh=nh, pz=pz: e.scalar_tensor_tensor(
                           out=zb[:, 0, nh * 512:(nh + 1) * 512],
                           in0=x_qt[:, 0, nh * 512:(nh + 1) * 512],
                           scalar=2.0, in1=pz[nh][:], op0=ALU.mult,
                           op1=ALU.add))
                OP("scalar", [zb], [s2, dumpb],
                   lambda e: e.activation(
                       out=dumpb[:], in_=zb[:, 0], func=AF.Square,
                       accum_out=s2[:, 0:1]))
                yield
                OP("vector", [zb], [s1],
                   lambda e: e.tensor_reduce(
                       out=s1[:, 0:1], in_=zb[:, 0], axis=AXL.X, op=ALU.add))
                OP("vector", [s1], [meanb],
                   lambda e: e.tensor_scalar(out=meanb[:, 0:1], in0=s1[:, 0:1],
                                             scalar1=1.0 / D, scalar2=None,
                                             op0=ALU.mult))
                OP("vector", [meanb], [varb],
                   lambda e: e.tensor_tensor(out=varb[:, 0:1], in0=meanb[:, 0:1],
                                             in1=meanb[:, 0:1], op=ALU.mult))
                OP("vector", [s2, varb], [varb],
                   lambda e: e.scalar_tensor_tensor(
                       out=varb[:, 0:1], in0=s2[:, 0:1], scalar=1.0 / D,
                       in1=varb[:, 0:1], op0=ALU.mult, op1=ALU.subtract))
                OP("scalar", [varb], [rstd1],
                   lambda e: e.activation(out=rstd1[:, 0:1], in_=varb[:, 0:1],
                                          func=AF.Ln, bias=EPS))
                OP("scalar", [rstd1], [rstd1],
                   lambda e: e.activation(out=rstd1[:, 0:1], in_=rstd1[:, 0:1],
                                          func=AF.Exp, scale=-0.5))
                yield
                OP("vector", [zb, meanb, rstd1], [zb],
                   lambda e: e.tensor_scalar(
                       out=zb[:, 0], in0=zb[:, 0],
                       scalar1=meanb[:, 0:1], scalar2=rstd1[:, 0:1],
                       op0=ALU.subtract, op1=ALU.mult))
                OP("scalar", [zb], [x1bb],
                   lambda e: e.activation(out=x1bb[:, 0], in_=zb[:, 0],
                                          func=AF.Identity))
                if debug and ci == 0:
                    DMA("sync", dbg["d_x1"], zb[:, 0, :], [zb], ["d_x1"])
                DMA("sync", x1_spill[ci * QT:(ci + 1) * QT, :], zb[:, 0, :],
                    [zb], ["x1_spill"])
                DMA("sync", x1b_spill[ci * QT:(ci + 1) * QT, :], x1bb[:, 0, :],
                    [x1bb], ["x1b_spill"])
                yield

            def drain(g, k=1):
                if g is not None:
                    for _ in range(k):
                        if next(g, "END") == "END":
                            return

            drain(genA(0), 99)
            emitGather(0)
            for m in range(NDC):
                emitDup(0, m)
                emitUY(0, m)
            emitBcast(0, 0)
            emitBcast(0, 1)
            slotA = {0: 1, 1: 1, 2: 1, 3: 1, 4: 1, 5: 1, 6: 1, 7: 1,
                     8: 1, 9: 1, 10: 1, 11: 1, 12: 1, 13: 1}
            slotP = {4: 1, 8: 1, 9: 1, 10: 1, 11: 1, 12: 1}
            gP = None
            for ci in range(EMIT_CHUNKS):
                gA = genA(ci + 1) if ci + 1 < EMIT_CHUNKS else None
                for n in range(NST):
                    if n == 2 and ci >= 1:
                        emitAGy(ci - 1)
                    drain(gP, slotP.get(n, 0))
                    drain(gA, slotA.get(n, 0))
                    if ci + 1 < EMIT_CHUNKS:
                        if n == 13:
                            emitGather(ci + 1)
                        elif n == 14:
                            emitDup(ci + 1, 0)
                            emitDup(ci + 1, 1)
                        elif n == 15:
                            emitDup(ci + 1, 2)
                            emitDup(ci + 1, 3)
                    emitScanGroup(ci, n)
                    if n + 2 < NST:
                        emitBcast(ci, n + 2)
                    elif ci + 1 < EMIT_CHUNKS:
                        emitBcast(ci + 1, n + 2 - NST)
                emitChunkTail(ci)
                drain(gP, 99)
                gP = genPost(ci)
            emitAGy(EMIT_CHUNKS - 1)
            drain(gP, 99)

        p.close_fence(nmM)
        if EMIT_FFN:
            with ExitStack() as esF:
                nmF = []
                sbF = tracked(esF, nmF)
                gbuf = sbF("gbuf", [128, 32, FSUB], BF16)
                t1_sb = sbF("t1_sb", [128, 512])
                ztmp = sbF("ztmp", [128, 512])
                out_t = sbF("out_t", [128, D])
                for sub in range(NFS):
                    R0 = sub * FSUB
                    if sub > 0:
                        p.close_fence([n + str(sub - 1) for n in
                                       ("w3rhs", "z2all", "x1row0", "x1row1")])
                    es123 = ExitStack()
                    nm123 = []
                    _sb123 = tracked(es123, nm123)
                    sb123 = lambda nme, shape, dt=BF16: _sb123(
                        nme + str(sub), shape, dt)
                    hobuf = sb123("hobuf", [128, 32, FSUB])
                    x1_fm = sb123("x1_fm", [128, 8, FSUB])
                    wblk = [sb123(f"wblk{i}", [128, 8, 128]) for i in range(4)]
                    with ExitStack() as esX:
                        x1tmb = esX.enter_context(
                            nc.sbuf_tensor(f"x1tmb{sub}", [128, 8, D], BF16))
                        DMA("sync", x1tmb[:],
                            x1b_spill[R0:R0 + FSUB, :].rearrange(
                                "(j p) d -> p j d", p=128), ["x1b_spill"], [x1tmb])
                        for dc8 in range(8):
                            for j in range(8):
                                pt = p_tr[(dc8 * 8 + j) % 2]
                                OP("tensor", [x1tmb, ident], [pt],
                                   lambda e, j=j, dc8=dc8, pt=pt: e.transpose(
                                       pt[:, 0:128], x1tmb[:, j, dc8 * 128:(dc8 + 1) * 128],
                                       ident[:]))
                                OP("scalar", [pt], [x1_fm],
                                   lambda e, j=j, dc8=dc8, pt=pt: e.activation(
                                       out=x1_fm[:, dc8, j * 128:(j + 1) * 128],
                                       in_=pt[:, 0:128], func=AF.Identity))
                    p.close_fence([f"x1tmb{sub}"])
                    for m in range(32):
                        wb1, wb2 = wblk[2 * (m % 2)], wblk[2 * (m % 2) + 1]
                        DMA("gpsimd", wb1[:],
                            w1t[:, m * 128:(m + 1) * 128].rearrange(
                                "(k p) m -> p k m", p=128), ["w1t"], [wb1])
                        DMA("gpsimd", wb2[:],
                            w2t[:, m * 128:(m + 1) * 128].rearrange(
                                "(k p) m -> p k m", p=128), ["w2t"], [wb2])
                        for tt in range(2):
                            pa, pb = p_mm[tt], p_mm[2 + tt]

                            def g1(e, wb1=wb1, tt=tt, pa=pa):
                                r = None
                                for k in range(8):
                                    r = e.matmul(
                                        pa[:], wb1[:, k, :],
                                        x1_fm[:, k, tt * 512:(tt + 1) * 512],
                                        start=(k == 0), stop=(k == 7))
                                return r

                            def g2(e, wb2=wb2, tt=tt, pb=pb):
                                r = None
                                for k in range(8):
                                    r = e.matmul(
                                        pb[:], wb2[:, k, :],
                                        x1_fm[:, k, tt * 512:(tt + 1) * 512],
                                        start=(k == 0), stop=(k == 7))
                                return r
                            OP("tensor", [wb1, x1_fm], [pa], g1)
                            OP("tensor", [wb2, x1_fm], [pb], g2)
                            OP("scalar", [pa, w1s_sb], [t1_sb],
                               lambda e, m=m, pa=pa: e.activation(
                                   out=t1_sb[:], in_=pa[:], func=AF.Silu,
                                   scale=w1s_sb[:, m:m + 1]))
                            OP("vector", [pb, w2s_sb, t1_sb], [gbuf],
                               lambda e, m=m, tt=tt, pb=pb: e.scalar_tensor_tensor(
                                   out=gbuf[:, m, tt * 512:(tt + 1) * 512],
                                   in0=pb[:], scalar=w2s_sb[:, m:m + 1],
                                   in1=t1_sb[:], op0=ALU.mult, op1=ALU.mult))
                    if debug and sub == 0:
                        DMA("sync", dbg["d_x1s"], x1b_spill, ["x1b_spill"], ["d_x1s"])
                        DMA("sync", dbg["d_g"],
                            gbuf[:].rearrange("p a b -> p (a b)"), [gbuf], ["d_g"])
                    for st_i, (wt_ap, wt_name, ssb, src, dst) in enumerate([
                            (h1t, "h1t", h1s_sb, gbuf, hobuf),
                            (h2t, "h2t", h2s_sb, hobuf, gbuf)]):
                        with ExitStack() as esH:
                            hblk = [esH.enter_context(nc.sbuf_tensor(
                                f"hblk{sub}_{st_i}_{i}", [128, 32, 128], BF16))
                                for i in range(2)]
                            for m in range(32):
                                hb = hblk[m % 2]
                                DMA("gpsimd", hb[:],
                                    wt_ap[:, m * 128:(m + 1) * 128].rearrange(
                                        "(k p) m -> p k m", p=128), [wt_name], [hb])
                                for tt in range(2):
                                    pa = p_mm[tt]

                                    def hg(e, hb=hb, tt=tt, pa=pa, src=src):
                                        r = None
                                        for k in range(32):
                                            r = e.matmul(
                                                pa[:], hb[:, k, :],
                                                src[:, k, tt * 512:(tt + 1) * 512],
                                                start=(k == 0), stop=(k == 31))
                                        return r
                                    OP("tensor", [hb, src], [pa], hg)
                                    OP("scalar", [pa, ssb], [dst],
                                       lambda e, m=m, tt=tt, pa=pa, ssb=ssb,
                                       dst=dst: e.activation(
                                           out=dst[:, m, tt * 512:(tt + 1) * 512],
                                           in_=pa[:], func=AF.Gelu,
                                           scale=ssb[:, m:m + 1]))
                        p.close_fence([f"hblk{sub}_{st_i}_{i}"
                                       for i in range(2)])
                    if debug and sub == 0:
                        DMA("sync", dbg["d_h1"],
                            hobuf[:].rearrange("p a b -> p (a b)"), [hobuf], ["d_h1"])
                        DMA("sync", dbg["d_h2"],
                            gbuf[:].rearrange("p a b -> p (a b)"), [gbuf], ["d_h2"])
                    es123.close()
                    p.close_fence(nm123)
                    with ExitStack() as esW:
                        nmW = []
                        _sbW = tracked(esW, nmW)
                        sbW = lambda nme, shape, dt=F32: _sbW(
                            nme + str(sub), shape, dt)
                        w3rhs = sbW("w3rhs", [128, 32, 512], BF16)
                        z2all = sbW("z2all", [128, 8, D])
                        x1row = [sbW(f"x1row{i}", [128, D]) for i in range(2)]
                        for nh in range(2):
                            DMA("gpsimd", w3rhs[:],
                                w3t[:, nh * 512:(nh + 1) * 512].rearrange(
                                    "(k p) m -> p k m", p=128), ["w3t"], [w3rhs])
                            for tc in range(8):
                                pzn = p_mm[tc % 2]

                                def w3g(e, tc=tc, pzn=pzn):
                                    r = None
                                    for k in range(32):
                                        r = e.matmul(
                                            pzn[:],
                                            gbuf[:, k, tc * 128:(tc + 1) * 128],
                                            w3rhs[:, k, :],
                                            start=(k == 0), stop=(k == 31))
                                    return r
                                OP("tensor", [gbuf, w3rhs], [pzn], w3g)
                                xr = x1row[tc % 2]
                                OP("sync", ["x1_spill"], [xr],
                                   lambda e, tc=tc, R0=R0, xr=xr: e.dma_start(
                                       out=xr[:],
                                       in_=x1_spill[R0 + tc * 128:
                                                    R0 + (tc + 1) * 128, :]),
                                   dma=True)
                                OP("vector", [pzn, w3sRep], [ztmp],
                                   lambda e, nh=nh, pzn=pzn: e.tensor_tensor(
                                       out=ztmp[:], in0=pzn[:],
                                       in1=w3sRep[:, nh * 512:(nh + 1) * 512],
                                       op=ALU.mult))
                                OP("vector", [ztmp, xr], [z2all],
                                   lambda e, tc=tc, nh=nh, xr=xr: e.tensor_tensor(
                                       out=z2all[:, tc, nh * 512:(nh + 1) * 512],
                                       in0=ztmp[:],
                                       in1=xr[:, nh * 512:(nh + 1) * 512],
                                       op=ALU.add))
                        if debug and sub == 0:
                            DMA("sync", dbg["d_z2"],
                                z2all[:].rearrange("p a b -> p (a b)"),
                                [z2all], ["d_z2"])
                        for tc in range(8):
                            OP("scalar", [z2all], [s2, dumpb],
                               lambda e, tc=tc: e.activation(
                                   out=dumpb[:], in_=z2all[:, tc], func=AF.Square,
                                   accum_out=s2[:, tc:tc + 1]))
                            OP("vector", [z2all], [s1],
                               lambda e, tc=tc: e.tensor_reduce(
                                   out=s1[:, tc:tc + 1], in_=z2all[:, tc],
                                   axis=AXL.X, op=ALU.add))
                        OP("vector", [s1], [meanb],
                           lambda e: e.tensor_scalar(
                               out=meanb[:], in0=s1[:], scalar1=1.0 / D,
                               scalar2=None, op0=ALU.mult))
                        OP("vector", [meanb], [varb],
                           lambda e: e.tensor_tensor(
                               out=varb[:], in0=meanb[:], in1=meanb[:],
                               op=ALU.mult))
                        OP("vector", [s2, varb], [varb],
                           lambda e: e.scalar_tensor_tensor(
                               out=varb[:], in0=s2[:], scalar=1.0 / D,
                               in1=varb[:], op0=ALU.mult, op1=ALU.subtract))
                        OP("scalar", [varb], [rstd1],
                           lambda e: e.activation(out=rstd1[:], in_=varb[:],
                                                  func=AF.Ln, bias=EPS))
                        OP("scalar", [rstd1], [rstd1],
                           lambda e: e.activation(out=rstd1[:], in_=rstd1[:],
                                                  func=AF.Exp, scale=-0.5))
                        if debug:
                            DMA("sync", dbg["d_st"],
                                s1[:], [s1], ["d_st"])
                            DMA("sync", dbg["d_st2"],
                                s2[:], [s2], ["d_st2"])
                            DMA("sync", dbg["d_rs"],
                                rstd1[:], [rstd1], ["d_rs"])
                        for tc in range(8):
                            OP("vector", [z2all, meanb, rstd1], [out_t],
                               lambda e, tc=tc: e.tensor_scalar(
                                   out=out_t[:], in0=z2all[:, tc],
                                   scalar1=meanb[:, tc:tc + 1],
                                   scalar2=rstd1[:, tc:tc + 1],
                                   op0=ALU.subtract, op1=ALU.mult))
                            DMA("sync", out[R0 + tc * 128:R0 + (tc + 1) * 128, :],
                                out_t[:], [out_t], ["out"])

        p.fence()
        snap = dict(p.cum)
        for e in ENGS:
            p.op(e, lambda en: en.nop(), reads=(), writes=())

        streams = p.finalize()

        def emit(engname):
            def f(e):
                for fn, inc, ch, waits in streams[engname]:
                    for (sch, sv) in waits:
                        e.wait_ge(sems[sch], sv)
                    instr = fn(e)
                    if instr is not None:
                        instr.then_inc(sems[ch], inc)
            return f

        block.sync(emit("sync"))
        block.scalar(emit("scalar"))
        block.vector(emit("vector"))
        block.tensor(emit("tensor"))
        block.gpsimd(emit("gpsimd"))

    return nc


def _stage_inputs(inputs, c):
    g, q = c // 4, c % 4
    d0 = q * DSL
    asc = np.ascontiguousarray
    x = np.asarray(inputs["L0_attn_x"])
    d = {}
    d["x_seq"] = asc(x[g])
    d["x_q"] = asc(x[g].reshape(NCHUNK, 4, QT, D)[:, q])
    wi_full = np.asarray(inputs["L0_attn_in_proj_w"])
    d["wi"] = asc(np.concatenate(
        [wi_full[:, d0:d0 + DSL], wi_full[:, DI + d0:DI + d0 + DSL]], axis=1))
    cw = np.asarray(inputs["L0_attn_conv_w"])[d0:d0 + DSL]
    d["convw"] = asc(cw.reshape(NDC, 128, KC).transpose(1, 0, 2).reshape(128, NDC * KC))
    cb = np.asarray(inputs["L0_attn_conv_b"])[d0:d0 + DSL]
    d["convb"] = asc(cb.reshape(NDC, 128).T)
    d["bcdw"] = asc(np.asarray(inputs["L0_attn_bcd_w"])[d0:d0 + DSL])
    d["dupw"] = asc(np.asarray(inputs["L0_attn_dup_w"])[:, d0:d0 + DSL])
    db = np.asarray(inputs["L0_attn_dup_b"])[d0:d0 + DSL]
    d["dupb"] = asc(db.reshape(NDC, 128).T)
    lm = np.asarray(inputs["L0_attn_log_minus_A"])[d0:d0 + DSL]
    d["lmA"] = asc(lm.reshape(NDC, 128, NST).transpose(1, 0, 2).reshape(128, NDC * NST))
    dp = np.asarray(inputs["L0_attn_D_param"])[d0:d0 + DSL]
    d["dparam"] = asc(dp.reshape(NDC, 128).T)
    rw = np.asarray(inputs["L0_attn_rms_w"])
    d["rmsw"] = asc(rw.reshape(8, 128).T)
    d["wout"] = asc(np.asarray(inputs["L0_attn_out_proj_w"]))
    d["w1t"] = asc(np.asarray(inputs["L0_ffn_w1_w"]).T)
    d["w2t"] = asc(np.asarray(inputs["L0_ffn_w2_w"]).T)
    d["h1t"] = asc(np.asarray(inputs["L0_ffn_h1_w"]).T)
    d["h2t"] = asc(np.asarray(inputs["L0_ffn_h2_w"]).T)
    d["w3t"] = asc(np.asarray(inputs["L0_ffn_w3_w"]).T)
    for nme, key in [("w1s", "L0_ffn_w1_s"), ("w2s", "L0_ffn_w2_s"),
                     ("h1s", "L0_ffn_h1_s"), ("h2s", "L0_ffn_h2_s")]:
        s = np.asarray(inputs[key])
        d[nme] = asc(s.reshape(32, 128).T)
    d["w3s"] = asc(np.asarray(inputs["L0_ffn_w3_s"]).reshape(1, D))
    d["ident_in"] = np.eye(128, dtype=np.float32)
    return d


def kernel(**inputs):
    nc = build(debug=DEBUG)
    in_maps = [_stage_inputs(inputs, c) for c in range(8)]
    res = run_bass_kernel_spmd(nc, in_maps, core_ids=list(range(8)))
    full = np.zeros((B, L, D), np.float32)
    for c in range(8):
        g, q = c // 4, c % 4
        oc = np.asarray(res.results[c]["out"])
        oc = oc.reshape(NCHUNK, QT, D)
        for ci in range(NCHUNK):
            full[g, ci * LC + q * QT:ci * LC + (q + 1) * QT, :] = oc[ci]
    if DEBUG:
        kernel.last_results = res
    return full
